# version 55
# baseline (speedup 1.0000x reference)
import numpy as np
import ml_dtypes
from contextlib import ExitStack

import concourse.bass as bass
import concourse.mybir as mybir
from concourse.bass_utils import run_bass_kernel_spmd

F32 = mybir.dt.float32
BF16 = mybir.dt.bfloat16
AF = mybir.ActivationFunctionType
ALU = mybir.AluOpType
AX = mybir.AxisListType
AP = bass.AP

B, S, E = 2, 8192, 2048
H, KVH, D = 16, 4, 128
WIN = 1024
CAP = 50.0
EPS_ATT = 1e-6
EPS_LN = 1e-5
NCORE = 8
TOWN = 2048
HALO = 4096
NT_OWN = TOWN // 128
NT_HALO = HALO // 128
NCH = 17
NPAIR = 9
NEXP, RANK = 8, 8
SCALE_MOE = 2.0


def _f32(x):
    return np.ascontiguousarray(x, dtype=np.float32)


def _mask_kind(qt, c):
    tri = "L" if c == 0 else ("R" if c == 16 else None)
    vb = (qt + c <= 7) or (qt + c >= 24)
    return tri, vb


class Ctr:
    def __init__(self, sem):
        self.sem = sem
        self.n = 0

    def inc(self, ins):
        ins.then_inc(self.sem, 1)
        self.n += 1
        return self.n


def build_program(upto="full", debug=False):
    nc = bass.Bass("TRN2", target_bir_lowering=False)

    xh = nc.dram_tensor("xh", [HALO, E], F32, kind="ExternalInput").ap()
    cos_in = nc.dram_tensor("cos_t", [HALO, D], F32, kind="ExternalInput").ap()
    sin_in = nc.dram_tensor("sin_t", [HALO, D], F32, kind="ExternalInput").ap()
    vb_in = nc.dram_tensor("vb", [128, NT_OWN * NCH], F32, kind="ExternalInput").ap()
    wq_in = nc.dram_tensor("wq", [E, H * D], F32, kind="ExternalInput").ap()
    wk_in = nc.dram_tensor("wk", [E, KVH * D], F32, kind="ExternalInput").ap()
    wv_in = nc.dram_tensor("wv", [E, KVH * D], F32, kind="ExternalInput").ap()
    wo_in = nc.dram_tensor("wo", [H * D, E], F32, kind="ExternalInput").ap()
    wbra_in = nc.dram_tensor("wbra", [E, 2120], F32, kind="ExternalInput").ap()
    bflat_in = nc.dram_tensor("bflat", [NEXP * RANK, E], F32, kind="ExternalInput").ap()
    bb_in = nc.dram_tensor("b_bcast", [1, E], F32, kind="ExternalInput").ap()
    qs_in = nc.dram_tensor("qscale_rep", [128, H * D], F32, kind="ExternalInput").ap()
    ks_in = nc.dram_tensor("kscale_rep", [128, KVH * D], F32, kind="ExternalInput").ap()

    scr_o = nc.dram_tensor("scr_o", [NT_OWN, 128, H * 128], BF16, kind="Internal").ap()
    out = nc.dram_tensor("out", [TOWN, E], F32, kind="ExternalOutput").ap()

    ident_np = np.eye(128, dtype=np.float32).astype(ml_dtypes.bfloat16)
    jj_ = np.arange(128)[:, None]
    ii_ = np.arange(128)[None, :]
    maskL_np = np.tile((jj_ >= ii_).astype(np.float32), (1, 4)).astype(ml_dtypes.bfloat16)
    maskR_np = np.tile((jj_ <= ii_).astype(np.float32), (1, 4)).astype(ml_dtypes.bfloat16)
    ident_h = nc.inline_tensor(ident_np, name="c_ident").ap()
    ones_h = nc.inline_tensor(np.ones((1, 128), dtype=np.float32).astype(ml_dtypes.bfloat16),
                              name="c_ones").ap()
    maskL_h = nc.inline_tensor(maskL_np, name="c_maskL").ap()
    maskR_h = nc.inline_tensor(maskR_np, name="c_maskR").ap()

    dbg = {}
    if debug:
        dbg["K"] = nc.dram_tensor("dbg_K", [128, KVH, HALO], BF16, kind="ExternalOutput").ap()
        dbg["V"] = nc.dram_tensor("dbg_V", [128, NT_HALO, KVH, 129], BF16, kind="ExternalOutput").ap()
        dbg["Q"] = nc.dram_tensor("dbg_Q", [128, H, TOWN], BF16, kind="ExternalOutput").ap()
        dbg["O"] = nc.dram_tensor("dbg_O", [NT_OWN, 128, H * 128], BF16, kind="ExternalOutput").ap()
        dbg["XH"] = nc.dram_tensor("dbg_XH", [128, E], BF16, kind="ExternalOutput").ap()
        dbg["QR"] = nc.dram_tensor("dbg_QR", [128, 1024], F32, kind="ExternalOutput").ap()

    st_top = ExitStack()
    st_kvq = ExitStack()
    st_p12 = ExitStack()
    st_p1 = ExitStack()
    st_p2 = ExitStack()
    st_p3 = ExitStack()
    st_p45 = ExitStack()

    en = st_top.enter_context
    ident = en(nc.sbuf_tensor("ident", [128, 128], BF16))
    rs_all = en(nc.sbuf_tensor("rs_all", [128, NT_HALO], F32))
    m_all = en(nc.sbuf_tensor("m_all", [128, NT_HALO], F32))
    eps_ln = en(nc.sbuf_tensor("eps_ln", [128, 1], F32))
    xbuf = en(nc.sbuf_tensor("xbuf", [128, 2, E], F32))
    xhat = en(nc.sbuf_tensor("xhat", [128, E], BF16))

    ek = st_kvq.enter_context
    K_sb = ek(nc.sbuf_tensor("K_sb", [128, KVH, HALO], BF16))
    Vp_sb = ek(nc.sbuf_tensor("Vp_sb", [128, NT_HALO, KVH, 129], BF16))
    Qfm = ek(nc.sbuf_tensor("Qfm", [128, H, TOWN], BF16))

    e12 = st_p12.enter_context
    wstg = e12(nc.sbuf_tensor("wstg", [128, 3, 512], F32))
    xT = e12(nc.sbuf_tensor("xT", [128, E], BF16))
    tbl = e12(nc.sbuf_tensor("tbl", [128, 3, 2, D], F32))
    rp_rot = e12(nc.sbuf_tensor("rp_rot", [128, 2, 512], F32))
    rp_t1 = e12(nc.sbuf_tensor("rp_t1", [128, 2, 512], F32))
    rp_ss = e12(nc.sbuf_tensor("rp_ss", [128, 2, 4], F32))
    rp_f = e12(nc.sbuf_tensor("rp_f", [128, 2, 4], F32))
    qk_bf = e12(nc.sbuf_tensor("qk_bf", [128, 2, 512], BF16))
    ps_tr = [e12(nc.psum_tensor(f"ps_tr{k}", [128, E], BF16)) for k in range(2)]

    d_x = en(nc.semaphore("d_x"))
    d_tb = en(nc.semaphore("d_tb"))
    d_w = en(nc.semaphore("d_w"))
    d_w2 = en(nc.semaphore("d_w2"))
    d_y1 = en(nc.semaphore("d_y1"))
    d_sc = en(nc.semaphore("d_sc"))
    d_out = en(nc.semaphore("d_out"))
    d_c = en(nc.semaphore("d_c"))
    s_pe = en(nc.semaphore("s_pe"))
    s_act = en(nc.semaphore("s_act"))
    s_dve = en(nc.semaphore("s_dve"))
    s_pool = en(nc.semaphore("s_pool"))

    block = en(nc.Block())

    PE = Ctr(s_pe)
    ACT = Ctr(s_act)
    DVE = Ctr(s_dve)
    POOL = Ctr(s_pool)
    dma = {"x": 0, "tb": 0, "w": 0, "w2": 0, "sc": 0, "out": 0, "c": 0, "y1": 0}

    def fence():
        marks = [(s_pe, PE.n), (s_act, ACT.n), (s_dve, DVE.n), (s_pool, POOL.n)]
        dmas = [(d_x, dma["x"]), (d_tb, dma["tb"]), (d_w, dma["w"]),
                (d_w2, dma["w2"]), (d_sc, dma["sc"]), (d_out, dma["out"]),
                (d_c, dma["c"]), (d_y1, dma["y1"])]
        for name in ("tensor", "scalar", "vector", "gpsimd", "sync"):
            @getattr(block, name)
            def _(eng):
                for sem, v in marks + dmas:
                    if v:
                        eng.wait_ge(sem, v)

    def waits(eng, guards):
        for sem, v in guards:
            if v:
                eng.wait_ge(sem, v)

    stage_hist = []

    def stream_weight(pieces, staging=None, nslots=3):
        stg = staging if staging is not None else wstg
        for src, dst in pieces:
            slot = len(stage_hist) % nslots
            npart = src.shape[0]
            ncol = src.shape[-1]
            h = (ncol // 2) if ncol >= 128 else ncol
            wvals = []
            for (qname, sem, key), (c0, c1) in zip(
                    (("gpsimd", d_w, "w"), ("scalar", d_w2, "w2")),
                    ((0, h), (h, ncol))):
                if c0 == c1:
                    continue

                @getattr(block, qname)
                def _(eng, slot=slot, src=src, c0=c0, c1=c1, npart=npart, sem=sem, key=key):
                    if len(stage_hist) >= nslots - 1:
                        eng.wait_ge(s_act, stage_hist[-(nslots - 1)])
                    eng.dma_start(out=stg[0:npart, slot, c0:c1],
                                  in_=src[:, c0:c1]).then_inc(sem, 16)
                    dma[key] += 16
                wvals.append((sem, dma[key]))

            @block.scalar
            def _(scalar, slot=slot, dst=dst, ncol=ncol, npart=npart, wvals=wvals):
                for wsem, wv in wvals:
                    scalar.wait_ge(wsem, wv)
                ins = scalar.activation(dst, stg[0:npart, slot, 0:ncol], AF.Copy)
                stage_hist.append(ACT.inc(ins))
        return stage_hist[-1]

    e1 = st_p1.enter_context
    wkv = e1(nc.sbuf_tensor("wkv", [128, 16, 1024], BF16))
    ks_sb = e1(nc.sbuf_tensor("ks_sb", [128, KVH * D], F32))
    ps_kv = [e1(nc.psum_tensor(f"ps_kv{k}", [128, 1024], F32)) for k in range(2)]

    @block.sync
    def _(sync):
        for dst, src in [(ident, ident_h), (ks_sb, ks_in)]:
            sync.dma_start(out=dst[:], in_=src).then_inc(d_c, 16)
            dma["c"] += 16
    const_dma = dma["c"]

    wk_r = wk_in.rearrange("(c p) f -> c p f", p=128)
    wv_r = wv_in.rearrange("(c p) f -> c p f", p=128)
    pieces = []
    for c in range(16):
        pieces.append((wk_r[c], wkv[:, c, 0:512]))
        pieces.append((wv_r[c], wkv[:, c, 512:1024]))
    wkv_ready = stream_weight(pieces)

    @block.vector
    def _(vector):
        ins = vector.memset(Vp_sb[:, :, :, 128:129], 1.0)
        DVE.inc(ins)
    vp_ones = DVE.n

    def tbl_aps(tslot):
        cosb = AP(tensor=tbl[:].tensor, offset=tbl[:].offset + tslot * 2 * D,
                  ap=[tbl[:].ap[0], [0, 4], [1, D]])
        sinb = AP(tensor=tbl[:].tensor, offset=tbl[:].offset + tslot * 2 * D + D,
                  ap=[tbl[:].ap[0], [0, 4], [1, D]])
        return cosb, sinb

    def v4(ap2d):
        return ap2d.rearrange("p (h d) -> p h d", h=4, d=D)

    def rope_ops(kp, rslot, tslot):
        kps = kp.rearrange("p (h s d) -> p h s d", h=4, s=2, d=64)
        rot_in = AP(tensor=kps.tensor, offset=kps.offset + 64,
                    ap=[kps.ap[0], kps.ap[1], [-64, 2], [1, 64]])
        rot = rp_rot[:, rslot, :]
        t1 = rp_t1[:, rslot, :]
        cosb, sinb = tbl_aps(tslot)
        yield lambda v: v.tensor_copy(rot.rearrange("p (h s d) -> p h s d", h=4, s=2, d=64), rot_in)
        yield lambda v: v.tensor_tensor(v4(t1), v4(kp), cosb, ALU.mult)
        yield lambda v: v.tensor_tensor(v4(rot), v4(rot), sinb, ALU.mult)
        yield lambda v: v.tensor_add(t1, t1, rot)
        yield lambda v: v.tensor_tensor(rot, t1, t1, ALU.mult)
        yield lambda v: v.tensor_reduce(rp_ss[:, rslot, :], v4(rot), AX.X, ALU.add)

    def apply_ops(rslot, scale_ap, qk_dst):
        f = rp_f[:, rslot, :]
        t1 = rp_t1[:, rslot, :]
        rot = rp_rot[:, rslot, :]
        fb = AP(tensor=f.tensor, offset=f.offset, ap=[f.ap[0], [1, 4], [0, D]])
        yield lambda v: v.tensor_scalar(f, f, EPS_ATT, None, ALU.add)
        yield lambda v: v.reciprocal(f, f)
        yield lambda v: v.tensor_tensor(v4(rot), v4(t1), fb, ALU.mult)
        yield lambda v: v.tensor_tensor(qk_dst, rot, scale_ap, ALU.mult)

    def emit_rope1(vector, kp, rslot, tslot):
        ops = list(rope_ops(kp, rslot, tslot))
        ops[0](vector)
        ops[1](vector)
        vector.drain()
        ops[2](vector)
        vector.drain()
        ops[3](vector)
        vector.drain()
        ops[4](vector)
        vector.drain()
        return ops[5](vector)

    def emit_apply(vector, rslot, scale_ap):
        ops = list(apply_ops(rslot, scale_ap, qk_bf[:, rslot, :]))
        ops[0](vector)
        vector.drain()
        ops[1](vector)
        vector.drain()
        ops[2](vector)
        vector.drain()
        return ops[3](vector)

    @block.vector
    def _(vector):
        ins = vector.memset(eps_ln[:], EPS_LN)
        DVE.inc(ins)

    pp_sum = e1(nc.sbuf_tensor("pp_sum", [128, NT_HALO], F32))
    pp_ss = e1(nc.sbuf_tensor("pp_ss", [128, NT_HALO], F32))
    pp_dma = [0] * NT_HALO
    pp_val = [0] * NT_HALO
    for i in range(NT_HALO):
        slot = i % 2

        @block.sync
        def _(sync, i=i, slot=slot):
            if i >= 2:
                sync.wait_ge(s_dve, pp_val[i - 2])
            sync.dma_start(out=xbuf[:, slot, :], in_=xh[i * 128:(i + 1) * 128, :]).then_inc(d_x, 16)
            dma["x"] += 16
            pp_dma[i] = dma["x"]

        @block.vector
        def _(vector, i=i, slot=slot):
            vector.wait_ge(d_x, pp_dma[i])
            vector.tensor_scalar(xhat[:], xbuf[:, slot, :], 1.0, 0.0, ALU.mult,
                                 ALU.add, accum_out=pp_sum[:, i:i + 1])
            ins = vector.scalar_tensor_tensor(xhat[:], xbuf[:, slot, :], 1.0,
                                              xbuf[:, slot, :], ALU.mult, ALU.mult,
                                              accum_out=pp_ss[:, i:i + 1])
            pp_val[i] = DVE.inc(ins)

    @block.vector
    def _(vector):
        vector.wait_ge(s_dve, pp_val[NT_HALO - 1])
        vector.drain()
        vector.tensor_scalar(m_all[:], pp_sum[:], 1.0 / E, None, ALU.mult)
        vector.tensor_scalar(pp_ss[:], pp_ss[:], 1.0 / E, None, ALU.mult)
        vector.drain()
        vector.tensor_tensor(pp_sum[:], m_all[:], m_all[:], ALU.mult)
        vector.drain()
        ins = vector.tensor_sub(pp_sum[:], pp_ss[:], pp_sum[:])
        DVE.inc(ins)

    @block.scalar
    def _(scalar):
        scalar.wait_ge(s_dve, DVE.n)
        scalar.activation(pp_sum[:], pp_sum[:], AF.Ln, bias=eps_ln[:])
        scalar.drain()
        ins = scalar.activation(rs_all[:], pp_sum[:], AF.Exp, scale=-0.5)
        ACT.inc(ins)
    ln1_ready = ACT.n

    x_dma = [0] * NT_HALO
    tb_dma = [0] * NT_HALO
    a_val = [0] * (NT_HALO + 4)
    b_val = [0] * (NT_HALO + 4)
    c_val = [0] * (NT_HALO + 4)
    ktr_val = [0] * NT_HALO
    trpe_val = [0] * NT_HALO
    kvmm_val = [0] * NT_HALO
    kn_val = [0] * NT_HALO
    kv_done_last = 0

    NIT1 = NT_HALO + 3
    for it in range(NIT1):
        i_dma = it
        i_nrm = it - 1
        i_rp = it - 2
        i_fin = it - 3

        if i_dma < NT_HALO:
            @block.sync
            def _(sync, i=i_dma):
                slot = i % 2
                if i < 2:
                    sync.wait_ge(s_dve, pp_val[NT_HALO - 2 + i])
                if i >= 2:
                    sync.wait_ge(s_dve, a_val[i - 1])
                if i >= 3:
                    sync.wait_ge(s_dve, c_val[i - 1])
                sync.dma_start(out=xbuf[:, slot, :], in_=xh[i * 128:(i + 1) * 128, :]).then_inc(d_x, 16)
                dma["x"] += 16
                x_dma[i] = dma["x"]
                sync.dma_start(out=tbl[:, i % 3, 0, :], in_=cos_in[i * 128:(i + 1) * 128, :]).then_inc(d_tb, 16)
                sync.dma_start(out=tbl[:, i % 3, 1, :], in_=sin_in[i * 128:(i + 1) * 128, :]).then_inc(d_tb, 16)
                dma["tb"] += 32
                tb_dma[i] = dma["tb"]

        @block.vector
        def _(vector, it=it, i_fin=i_fin, i_nrm=i_nrm):
            ops = []
            if 0 <= i_fin < NT_HALO:
                vector.wait_ge(s_act, kn_val[i_fin])
                if i_fin >= 2:
                    vector.wait_ge(s_pe, ktr_val[i_fin - 2])
                rslot = i_fin % 2
                aps = list(apply_ops(rslot, ks_sb[:], qk_bf[:, rslot, :]))
                pslot = i_fin % 2
                vdst = AP(tensor=Vp_sb[:].tensor,
                          offset=Vp_sb[:].offset + i_fin * (KVH * 129),
                          ap=[Vp_sb[:].ap[0], [129, 4], [1, 128]])
                vcopy = lambda v: v.tensor_copy(
                    vdst, ps_kv[pslot][:, 512:1024].rearrange("p (h d) -> p h d", h=4, d=D))
                ops = [aps[0], vcopy, aps[1], aps[2], aps[3]]
            nrm = []
            if 0 <= i_nrm < NT_HALO:
                vector.wait_ge(d_x, x_dma[i_nrm])
                vector.wait_ge(s_act, ln1_ready)
                if i_nrm >= 1:
                    vector.wait_ge(s_pe, trpe_val[i_nrm - 1])
                slot = i_nrm % 2
                nrm = [
                    lambda v, slot=slot, i=i_nrm: v.tensor_scalar(
                        xbuf[:, slot, :], xbuf[:, slot, :], m_all[:, i:i + 1], None, ALU.subtract),
                    lambda v, slot=slot, i=i_nrm: v.tensor_scalar(
                        xhat[:], xbuf[:, slot, :], rs_all[:, i:i + 1], None, ALU.mult),
                ]
            if ops and nrm:
                seq = [ops[0], ops[1], ops[2], nrm[0], ops[3], nrm[1], ops[4]]
            elif ops:
                seq = [ops[0], ops[1], ops[2], None, ops[3], None, ops[4]]
            else:
                seq = [nrm[0], None, nrm[1]] if nrm else []
            ins = None
            for f in seq:
                if f is None:
                    vector.drain()
                else:
                    ins = f(vector)
            if ins is not None:
                a_val[it] = DVE.inc(ins)
            else:
                a_val[it] = DVE.n

        if 0 <= i_fin < NT_HALO:
            @block.tensor
            def _(tensor, it=it, i_fin=i_fin):
                tensor.wait_ge(s_dve, a_val[it])
                rslot = i_fin % 2
                for h in range(4):
                    ins = tensor.transpose(ps_tr[rslot][:, h * 128:(h + 1) * 128],
                                           qk_bf[:, rslot, h * 128:(h + 1) * 128], ident[:])
                ktr_val[i_fin] = PE.inc(ins)

            @block.vector
            def _(vector, it=it, i_fin=i_fin):
                vector.wait_ge(s_pe, ktr_val[i_fin])
                rslot = i_fin % 2
                kdst = AP(tensor=K_sb[:].tensor, offset=K_sb[:].offset + i_fin * 128,
                          ap=[K_sb[:].ap[0], [HALO, 4], [1, 128]])
                ins = vector.tensor_copy(
                    kdst, ps_tr[rslot][:, 0:512].rearrange("p (h d) -> p h d", h=4, d=128))
                b_val[it] = DVE.inc(ins)
        else:
            b_val[it] = DVE.n

        if 0 <= i_nrm < NT_HALO:
            @block.tensor
            def _(tensor, it=it, i_nrm=i_nrm):
                tensor.wait_ge(s_dve, a_val[it])
                tensor.wait_ge(s_dve, b_val[it])
                slot = i_nrm % 2
                for c in range(16):
                    ins = tensor.transpose(ps_tr[slot][:, c * 128:(c + 1) * 128],
                                           xhat[:, c * 128:(c + 1) * 128], ident[:])
                trpe_val[i_nrm] = PE.inc(ins)

        @block.vector
        def _(vector, it=it, i_rp=i_rp, i_nrm=i_nrm):
            ins = None
            rp = []
            if 0 <= i_rp < NT_HALO:
                vector.wait_ge(s_pe, kvmm_val[i_rp])
                vector.wait_ge(d_tb, tb_dma[i_rp])
                rp = list(rope_ops(ps_kv[i_rp % 2][:, 0:512], i_rp % 2, i_rp % 3))
                for f in rp[:3]:
                    ins = f(vector)
            if 0 <= i_nrm < NT_HALO:
                vector.wait_ge(s_pe, trpe_val[i_nrm])
                if i_nrm >= 1:
                    vector.wait_ge(s_pe, kvmm_val[i_nrm - 1])
                ins = vector.tensor_copy(xT[:], ps_tr[i_nrm % 2][:])
            elif rp:
                vector.drain()
            if rp:
                ins = rp[3](vector)
                vector.drain()
                ins = rp[4](vector)
                vector.drain()
                ins = rp[5](vector)
            if ins is not None:
                c_val[it] = DVE.inc(ins)
            else:
                c_val[it] = DVE.n

        if 0 <= i_nrm < NT_HALO:
            @block.tensor
            def _(tensor, it=it, i_nrm=i_nrm):
                tensor.wait_ge(s_dve, c_val[it])
                if i_nrm == 0:
                    tensor.wait_ge(s_act, wkv_ready)
                slot = i_nrm % 2
                for c in range(16):
                    lhsT = xT[:, c * 128:(c + 1) * 128]
                    tensor.matmul(ps_kv[slot][:, 0:512], lhsT, wkv[:, c, 0:512],
                                  start=(c == 0), stop=(c == 15))
                    ins = tensor.matmul(ps_kv[slot][:, 512:1024], lhsT, wkv[:, c, 512:1024],
                                        start=(c == 0), stop=(c == 15))
                kvmm_val[i_nrm] = PE.inc(ins)

        if 0 <= i_rp < NT_HALO:
            @block.scalar
            def _(scalar, it=it, i_rp=i_rp):
                scalar.wait_ge(s_dve, c_val[it])
                rslot = i_rp % 2
                scalar.activation(rp_f[:, rslot, :], rp_ss[:, rslot, :], AF.Ln)
                scalar.drain()
                ins = scalar.activation(rp_f[:, rslot, :], rp_f[:, rslot, :],
                                        AF.Exp, scale=0.5)
                kn_val[i_rp] = ACT.inc(ins)
    kv_done_last = b_val[NIT1 - 1]

    fence()
    st_p1.close()

    if debug:
        @block.sync
        def _(sync):
            sync.dma_start(out=dbg["K"], in_=K_sb[:]).then_inc(d_c, 16)
            sync.dma_start(out=dbg["V"], in_=Vp_sb[:]).then_inc(d_c, 16)
            dma["c"] += 32
            sync.wait_ge(d_c, dma["c"])
    if upto == "p1":
        st_p12.close()
        st_kvq.close()
        st_top.close()
        return nc

    fence()
    e2 = st_p2.enter_context
    wqh = e2(nc.sbuf_tensor("wqh", [128, 16, 1024], BF16))
    qs_sb = e2(nc.sbuf_tensor("qs_sb", [128, 2, 512], F32))
    ps_q = [e2(nc.psum_tensor(f"ps_q{k}", [128, 1024], F32)) for k in range(2)]
    wq_r = wq_in.rearrange("(c p) f -> c p f", p=128)

    for g in range(2):
        if g == 1:
            fence()
        pieces = []
        for c in range(16):
            for hcol in range(2):
                pieces.append((wq_r[c][:, g * 1024 + hcol * 512:g * 1024 + (hcol + 1) * 512],
                               wqh[:, c, hcol * 512:(hcol + 1) * 512]))
        n0 = len(stage_hist)
        stream_weight(pieces)
        wq_vals = stage_hist[n0:]

        @block.sync
        def _(sync, g=g):
            for sg in range(2):
                sync.dma_start(out=qs_sb[:, sg, :],
                               in_=qs_in[:, (g * 2 + sg) * 512:(g * 2 + sg + 1) * 512]).then_inc(d_c, 16)
                dma["c"] += 16
        qs_dma = dma["c"]

        x_dma2 = [0] * NT_OWN
        tb_dma2 = [0] * NT_OWN
        a2 = [0] * (NT_OWN + 4)
        b2 = [0] * (NT_OWN + 4)
        c2 = [0] * (NT_OWN + 4)
        xv = [0] * (NT_OWN + 4)
        qtr2 = [0] * NT_OWN
        trpe2 = [0] * NT_OWN
        qmm2 = [0] * NT_OWN
        kn2 = [0] * NT_OWN

        NIT2 = NT_OWN + 3
        for it in range(NIT2):
            i_dma = it
            i_nrm = it - 1
            i_rp = it - 2
            i_fin = it - 3

            if i_dma < NT_OWN:
                @block.sync
                def _(sync, i=i_dma, g=g):
                    slot = i % 2
                    hi = i + 8
                    if i >= 2:
                        sync.wait_ge(s_dve, a2[i - 1])
                    if i >= 3:
                        sync.wait_ge(s_dve, c2[i - 1])
                    sync.dma_start(out=xbuf[:, slot, :], in_=xh[hi * 128:(hi + 1) * 128, :]).then_inc(d_x, 16)
                    dma["x"] += 16
                    x_dma2[i] = dma["x"]
                    sync.dma_start(out=tbl[:, i % 3, 0, :], in_=cos_in[hi * 128:(hi + 1) * 128, :]).then_inc(d_tb, 16)
                    sync.dma_start(out=tbl[:, i % 3, 1, :], in_=sin_in[hi * 128:(hi + 1) * 128, :]).then_inc(d_tb, 16)
                    dma["tb"] += 32
                    tb_dma2[i] = dma["tb"]

            @block.vector
            def _(vector, it=it, i_fin=i_fin, i_nrm=i_nrm, g=g):
                ap0 = ap1 = None
                if 0 <= i_fin < NT_OWN:
                    vector.wait_ge(s_act, kn2[i_fin])
                    vector.wait_ge(d_c, qs_dma)
                    if i_fin >= 1:
                        vector.wait_ge(s_pe, qtr2[i_fin - 1])
                    ap0 = list(apply_ops(0, qs_sb[:, 0, :], qk_bf[:, 0, :]))
                    ap1 = list(apply_ops(1, qs_sb[:, 1, :], qk_bf[:, 1, :]))
                nrm = []
                if 0 <= i_nrm < NT_OWN:
                    vector.wait_ge(d_x, x_dma2[i_nrm])
                    if i_nrm >= 1:
                        vector.wait_ge(s_pe, trpe2[i_nrm - 1])
                    slot = i_nrm % 2
                    hi = i_nrm + 8
                    nrm = [
                        lambda v, slot=slot, hi=hi: v.tensor_scalar(
                            xbuf[:, slot, :], xbuf[:, slot, :], m_all[:, hi:hi + 1], None, ALU.subtract),
                        lambda v, slot=slot, hi=hi: v.tensor_scalar(
                            xhat[:], xbuf[:, slot, :], rs_all[:, hi:hi + 1], None, ALU.mult),
                    ]
                ins = None
                if ap0 and nrm:
                    seq = [ap0[0], ap1[0], nrm[0], ap0[1], ap1[1], nrm[1],
                           ap0[2], ap1[2], ap0[3], ap1[3]]
                elif ap0:
                    seq = [ap0[0], ap1[0], None, ap0[1], ap1[1], None,
                           ap0[2], ap1[2], ap0[3], ap1[3]]
                else:
                    seq = [nrm[0], None, nrm[1]] if nrm else []
                for f in seq:
                    if f is None:
                        vector.drain()
                    else:
                        ins = f(vector)
                a2[it] = DVE.inc(ins) if ins is not None else DVE.n

            if 0 <= i_fin < NT_OWN:
                @block.tensor
                def _(tensor, it=it, i_fin=i_fin):
                    tensor.wait_ge(s_dve, a2[it])
                    if it >= 1:
                        tensor.wait_ge(s_act, b2[it - 1])
                    for sg in range(2):
                        for h in range(4):
                            ins = tensor.transpose(
                                ps_tr[0][:, (sg * 4 + h) * 128:(sg * 4 + h + 1) * 128],
                                qk_bf[:, sg, h * 128:(h + 1) * 128], ident[:])
                    qtr2[i_fin] = PE.inc(ins)

                @block.scalar
                def _(scalar, it=it, i_fin=i_fin, g=g):
                    scalar.wait_ge(s_pe, qtr2[i_fin])
                    ins = None
                    for sg in range(2):
                        h0 = g * 8 + sg * 4
                        qdst = AP(tensor=Qfm[:].tensor,
                                  offset=Qfm[:].offset + h0 * TOWN + i_fin * 128,
                                  ap=[Qfm[:].ap[0], [TOWN, 4], [1, 128]])
                        ins = scalar.activation(
                            qdst, ps_tr[0][:, sg * 512:(sg + 1) * 512].rearrange("p (h d) -> p h d", h=4, d=128),
                            AF.Copy)
                    b2[it] = ACT.inc(ins)
            else:
                b2[it] = ACT.n

            if 0 <= i_nrm < NT_OWN:
                @block.tensor
                def _(tensor, it=it, i_nrm=i_nrm):
                    tensor.wait_ge(s_dve, a2[it])
                    if i_nrm >= 1:
                        tensor.wait_ge(s_act, xv[it - 1])
                    for c in range(16):
                        ins = tensor.transpose(ps_tr[1][:, c * 128:(c + 1) * 128],
                                               xhat[:, c * 128:(c + 1) * 128], ident[:])
                    trpe2[i_nrm] = PE.inc(ins)

                @block.scalar
                def _(scalar, it=it, i_nrm=i_nrm):
                    scalar.wait_ge(s_pe, trpe2[i_nrm])
                    if i_nrm >= 1:
                        scalar.wait_ge(s_pe, qmm2[i_nrm - 1])
                    scalar.activation(xT[:], ps_tr[1][:], AF.Copy)
                    ins = scalar.activation(xT[:, 0:128], ps_tr[1][:, 0:128], AF.Copy)
                    xv[it] = ACT.inc(ins)
            else:
                xv[it] = ACT.n

            @block.vector
            def _(vector, it=it, i_rp=i_rp, i_nrm=i_nrm):
                ins = None
                rp0 = rp1 = None
                if 0 <= i_rp < NT_OWN:
                    vector.wait_ge(s_pe, qmm2[i_rp])
                    vector.wait_ge(d_tb, tb_dma2[i_rp])
                    slot = i_rp % 2
                    rp0 = list(rope_ops(ps_q[slot][:, 0:512], 0, i_rp % 3))
                    rp1 = list(rope_ops(ps_q[slot][:, 512:1024], 1, i_rp % 3))
                    for k in range(3):
                        ins = rp0[k](vector)
                        ins = rp1[k](vector)
                    vector.drain()
                    ins = rp0[3](vector)
                    ins = rp1[3](vector)
                    ins = rp0[4](vector)
                    ins = rp1[4](vector)
                    ins = rp0[5](vector)
                    ins = rp1[5](vector)
                c2[it] = DVE.inc(ins) if ins is not None else DVE.n

            if 0 <= i_nrm < NT_OWN:
                @block.tensor
                def _(tensor, it=it, i_nrm=i_nrm, g=g):
                    tensor.wait_ge(s_act, xv[it])
                    if it >= 1:
                        tensor.wait_ge(s_dve, c2[it - 1])
                    slot = i_nrm % 2
                    for c in range(16):
                        if i_nrm == 0:
                            tensor.wait_ge(s_act, wq_vals[2 * c + 1])
                        lhsT = xT[:, c * 128:(c + 1) * 128]
                        tensor.matmul(ps_q[slot][:, 0:512], lhsT, wqh[:, c, 0:512],
                                      start=(c == 0), stop=(c == 15))
                        ins = tensor.matmul(ps_q[slot][:, 512:1024], lhsT, wqh[:, c, 512:1024],
                                            start=(c == 0), stop=(c == 15))
                    qmm2[i_nrm] = PE.inc(ins)

            if 0 <= i_rp < NT_OWN:
                @block.scalar
                def _(scalar, it=it, i_rp=i_rp):
                    scalar.wait_ge(s_dve, c2[it])
                    scalar.activation(rp_f[:, 0, :], rp_ss[:, 0, :], AF.Ln)
                    scalar.activation(rp_f[:, 1, :], rp_ss[:, 1, :], AF.Ln)
                    scalar.drain()
                    scalar.activation(rp_f[:, 0, :], rp_f[:, 0, :], AF.Exp, scale=0.5)
                    ins = scalar.activation(rp_f[:, 1, :], rp_f[:, 1, :], AF.Exp, scale=0.5)
                    kn2[i_rp] = ACT.inc(ins)

    fence()
    st_p2.close()
    st_p12.close()

    if debug:
        @block.sync
        def _(sync):
            sync.dma_start(out=dbg["Q"], in_=Qfm[:]).then_inc(d_c, 16)
            dma["c"] += 16
            sync.wait_ge(d_c, dma["c"])
    if upto == "p2":
        st_kvq.close()
        st_top.close()
        return nc

    e3 = st_p3.enter_context
    maskL = e3(nc.sbuf_tensor("maskL", [128, 512], BF16))
    maskR = e3(nc.sbuf_tensor("maskR", [128, 512], BF16))
    vb_sb = e3(nc.sbuf_tensor("vb_sb", [128, NT_OWN * NCH], F32))
    pt_ring = e3(nc.sbuf_tensor("pt_ring", [128, 12, 1024], BF16))
    otm = e3(nc.sbuf_tensor("otm", [128, 512], BF16))
    orc = e3(nc.sbuf_tensor("orc", [128, 4], F32))
    ofm = e3(nc.sbuf_tensor("ofm", [128, 2, H * 128], BF16))
    ps_st = [e3(nc.psum_tensor(f"ps_st{k}", [128, 1024], F32)) for k in range(2)]
    ps_pv = [e3(nc.psum_tensor(f"ps_pv{k}", [128, 258], F32)) for k in range(2)]
    ps_ot = e3(nc.psum_tensor("ps_ot", [128, 512], BF16))

    @block.sync
    def _(sync):
        for dst, src in [(maskL, maskL_h), (maskR, maskR_h), (vb_sb, vb_in)]:
            sync.dma_start(out=dst[:], in_=src).then_inc(d_c, 16)
            dma["c"] += 16
    masks_dma = dma["c"]

    def _emit_norm_tail(itpair):
        qt, kv = itpair

        @block.vector
        def _(vector, qt=qt, kv=kv):
            vector.wait_ge(s_pe, PE.n)
            vector.wait_ge(s_pe, st3["otm_pe"])
            for jj in range(2):
                t = ps_pv[jj][:]
                dsrc = AP(tensor=t.tensor, offset=t.offset + 128,
                          ap=[t.ap[0], [129, 2], [1, 1]])
                vector.tensor_copy(orc[:, jj * 2:(jj + 1) * 2], dsrc)
            vector.drain()
            vector.reciprocal(orc[:], orc[:])
            vector.drain()
            for jj in range(2):
                t = ps_pv[jj][:]
                osrc = AP(tensor=t.tensor, offset=t.offset,
                          ap=[t.ap[0], [129, 2], [1, 128]])
                rb = AP(tensor=orc[:].tensor, offset=orc[:].offset + jj * 2,
                        ap=[orc[:].ap[0], [1, 2], [0, 128]])
                ins = vector.tensor_tensor(
                    otm[:, jj * 256:(jj + 1) * 256].rearrange("p (h d) -> p h d", h=2, d=128),
                    osrc, rb, ALU.mult)
            DVE.inc(ins)
        st3["pv_norm"] = DVE.n

        @block.tensor
        def _(tensor):
            tensor.wait_ge(s_dve, st3["pv_norm"])
            tensor.wait_ge(s_dve, st3["ot_evac"])
            for hh in range(4):
                ins = tensor.transpose(ps_ot[:, hh * 128:(hh + 1) * 128],
                                       otm[:, hh * 128:(hh + 1) * 128], ident[:])
            PE.inc(ins)
        st3["otm_pe"] = PE.n

        @block.vector
        def _(vector, qt=qt, kv=kv):
            vector.wait_ge(s_pe, st3["otm_pe"])
            if kv == 0:
                waits(vector, g_ofm[qt % 2])
            ins = vector.tensor_copy(ofm[:, qt % 2, kv * 512:(kv + 1) * 512], ps_ot[:])
            DVE.inc(ins)
        st3["ot_evac"] = DVE.n

        if kv == KVH - 1:
            @block.sync
            def _(sync, qt=qt):
                sync.wait_ge(s_dve, st3["ot_evac"])
                sync.dma_start(out=scr_o[qt], in_=ofm[:, qt % 2, :]).then_inc(d_sc, 16)
                dma["sc"] += 16
            ofm_dma_val[qt] = dma["sc"]
            g_ofm[qt % 2] = [(d_sc, dma["sc"])]

    iters = [(qt, kv) for qt in range(NT_OWN) for kv in range(KVH)]
    NPT = 12
    st3 = {"pv_norm": 0, "otm_pe": 0, "ot_evac": 0}
    rec_exp = {}
    rec_stpe = {}
    rec_pvpe = {}
    rec_mask = {}
    ofm_dma_val = [0] * NT_OWN
    g_ofm = [[], []]

    def pcs(p):
        return [2 * p] + ([2 * p + 1] if 2 * p + 1 < NCH else [])

    for it, (qt, kv) in enumerate(iters):
        act_base = ACT.n

        @block.tensor
        def _(tensor, it=it, qt=qt, kv=kv, act_base=act_base):
            if it == 0:
                tensor.wait_ge(d_c, masks_dma)
            pqt, pkv = iters[it - 1] if it >= 1 else (None, None)
            for p in range(NPAIR):
                g = it * NPAIR + p
                sslot = g % 2
                if p >= 2:
                    tensor.wait_ge(s_act, act_base + p - 1)
                elif it >= 1:
                    tensor.wait_ge(s_act, rec_exp[(it - 1, NPAIR - 2 + p)])
                for c in pcs(p):
                    lhsT = K_sb[:, kv, (qt + c) * 128:(qt + c + 1) * 128]
                    rhs = AP(tensor=Qfm[:].tensor,
                             offset=Qfm[:].offset + (4 * kv) * TOWN + qt * 128,
                             ap=[Qfm[:].ap[0], [TOWN, 4], [1, 128]])
                    ins = tensor.matmul(ps_st[sslot][:, (c % 2) * 512:(c % 2 + 1) * 512],
                                        lhsT, rhs, start=True, stop=True)
                rec_stpe[(it, p)] = PE.inc(ins)
                if it >= 1:
                    pg = (it - 1) * NPAIR + p
                    pslot = pg % NPT
                    for c, req in rec_mask[(it - 1, p)]:
                        tensor.wait_ge(s_act, req["act"])
                        if "dve" in req:
                            tensor.wait_ge(s_dve, req["dve"])
                        if c == 0:
                            tensor.wait_ge(s_dve, st3["pv_norm"])
                        col = (c % 2) * 512
                        for hh in range(4):
                            lhsT = pt_ring[:, pslot, col + hh * 128:col + (hh + 1) * 128]
                            rhs = Vp_sb[:, pqt + c, pkv, :]
                            ins = tensor.matmul(
                                ps_pv[hh // 2][:, (hh % 2) * 129:(hh % 2 + 1) * 129],
                                lhsT, rhs, start=(c == 0 and hh % 2 == 0),
                                stop=(c == NCH - 1))
                    rec_pvpe[(it - 1, p)] = PE.inc(ins)

        if it >= 1:
            _emit_norm_tail(iters[it - 1])

        @block.scalar
        def _(scalar, it=it, act_base=act_base):
            for p in range(NPAIR):
                g = it * NPAIR + p
                sslot = g % 2
                pslot = g % NPT
                scalar.wait_ge(s_pe, rec_stpe[(it, p)])
                og, rem = divmod(g - NPT, NPAIR)
                if g >= NPT and (og, rem) in rec_pvpe:
                    scalar.wait_ge(s_pe, rec_pvpe[(og, rem)])
                ncol = len(pcs(p)) * 512
                scalar.activation(ps_st[sslot][:, 0:ncol], ps_st[sslot][:, 0:ncol],
                                  AF.Tanh, scale=1.0 / (CAP * float(np.sqrt(float(D)))))
                ins = scalar.activation(pt_ring[:, pslot, 0:ncol],
                                        ps_st[sslot][:, 0:ncol], AF.Exp, scale=CAP)
                v = ACT.inc(ins)
                assert v == act_base + p + 1
                rec_exp[(it, p)] = v

        @block.vector
        def _(vector, it=it, qt=qt):
            for p in range(NPAIR):
                g = it * NPAIR + p
                pslot = g % NPT
                auth = []
                for c in pcs(p):
                    tri, needs_vb = _mask_kind(qt, c)
                    col = (c % 2) * 512
                    pa = pt_ring[:, pslot, col:col + 512]
                    vbcol = vb_sb[:, qt * NCH + c:qt * NCH + c + 1]
                    if tri is not None or needs_vb:
                        vector.wait_ge(s_act, rec_exp[(it, p)])
                        if tri is not None and needs_vb:
                            m = maskL[:] if tri == "L" else maskR[:]
                            ins = vector.scalar_tensor_tensor(pa, pa, vbcol, m,
                                                              ALU.mult, ALU.mult)
                        elif tri is not None:
                            m = maskL[:] if tri == "L" else maskR[:]
                            ins = vector.tensor_tensor(pa, pa, m, ALU.mult)
                        else:
                            ins = vector.tensor_scalar(pa, pa, vbcol, None, ALU.mult)
                        auth.append((c, {"act": rec_exp[(it, p)], "dve": DVE.inc(ins)}))
                    else:
                        auth.append((c, {"act": rec_exp[(it, p)]}))
                rec_mask[(it, p)] = auth

    it = len(iters)

    @block.tensor
    def _(tensor):
        pqt, pkv = iters[-1]
        for p in range(NPAIR):
            pg = (it - 1) * NPAIR + p
            pslot = pg % NPT
            for c, req in rec_mask[(it - 1, p)]:
                tensor.wait_ge(s_act, req["act"])
                if "dve" in req:
                    tensor.wait_ge(s_dve, req["dve"])
                if c == 0:
                    tensor.wait_ge(s_dve, st3["pv_norm"])
                col = (c % 2) * 512
                for hh in range(4):
                    lhsT = pt_ring[:, pslot, col + hh * 128:col + (hh + 1) * 128]
                    rhs = Vp_sb[:, pqt + c, pkv, :]
                    ins = tensor.matmul(
                        ps_pv[hh // 2][:, (hh % 2) * 129:(hh % 2 + 1) * 129],
                        lhsT, rhs, start=(c == 0 and hh % 2 == 0),
                        stop=(c == NCH - 1))
            rec_pvpe[(it - 1, p)] = PE.inc(ins)

    _emit_norm_tail(iters[-1])

    fence()
    st_p3.close()
    st_kvq.close()

    if debug:
        for qt in range(NT_OWN):
            @block.sync
            def _(sync, qt=qt):
                sync.dma_start(out=dbg["O"][qt], in_=scr_o[qt]).then_inc(d_c, 16)
                dma["c"] += 16
        @block.sync
        def _(sync):
            sync.wait_ge(d_c, dma["c"])
    if upto == "p3":
        st_top.close()
        return nc

    scr_y1 = nc.dram_tensor("scr_y1", [NT_OWN, 128, E], F32, kind="Internal").ap()
    rs2 = en(nc.sbuf_tensor("rs2", [128, NT_OWN], F32))
    m2_ = en(nc.sbuf_tensor("m2_", [128, NT_OWN], F32))
    var2 = en(nc.sbuf_tensor("var2", [128, NT_OWN], F32))
    msq2 = en(nc.sbuf_tensor("msq2", [128, NT_OWN], F32))

    stW = ExitStack()
    eW = stW.enter_context
    wbra_sb = eW(nc.sbuf_tensor("wbra_sb", [128, 16, 2120], BF16))
    bflat_sb = eW(nc.sbuf_tensor("bflat_sb", [64, 2048], BF16))
    bb_stage = eW(nc.sbuf_tensor("bb_stage", [1, E], F32))
    bb_bf = eW(nc.sbuf_tensor("bb_bf", [1, E], BF16))
    onesc = eW(nc.sbuf_tensor("onesc", [1, 128], BF16))
    wstg45 = eW(nc.sbuf_tensor("wstg45", [128, 3, 1024], F32))

    stA = ExitStack()
    eA = stA.enter_context
    wo_sb = eA(nc.sbuf_tensor("wo_sb", [128, 16, 2048], BF16))
    ot_sb = eA(nc.sbuf_tensor("ot_sb", [128, 2, H * 128], BF16))
    y1b = eA(nc.sbuf_tensor("y1b", [128, 2, E], F32))
    stA_sum = eA(nc.sbuf_tensor("stA_sum", [128, NT_OWN], F32))
    stA_ss = eA(nc.sbuf_tensor("stA_ss", [128, NT_OWN], F32))
    ps_wo = [eA(nc.psum_tensor(f"ps_wo{k}", [128, E], F32)) for k in range(2)]

    wo_r = wo_in.rearrange("(c p) f -> c p f", p=128)
    pieces = []
    for c in range(16):
        for n in range(2):
            pieces.append((wo_r[c][:, n * 1024:(n + 1) * 1024], wo_sb[:, c, n * 1024:(n + 1) * 1024]))
    n0 = len(stage_hist)
    stream_weight(pieces, staging=wstg45, nslots=3)
    wo_vals = stage_hist[n0:]

    @block.sync
    def _(sync):
        for dst, src in [(bb_stage, bb_in), (onesc, ones_h)]:
            sync.dma_start(out=dst[:], in_=src).then_inc(d_c, 16)
            dma["c"] += 16
    bbdma = dma["c"]

    @block.scalar
    def _(scalar):
        scalar.wait_ge(d_c, bbdma)
        ins = scalar.activation(bb_bf[:], bb_stage[:], AF.Copy)
        ACT.inc(ins)
    bb_ready = ACT.n

    wbra_r = wbra_in.rearrange("(c p) f -> c p f", p=128)
    pieces = []
    for c in range(16):
        for n in range(2):
            pieces.append((wbra_r[c][:, n * 1024:(n + 1) * 1024], wbra_sb[:, c, n * 1024:(n + 1) * 1024]))
        pieces.append((wbra_r[c][:, 2048:2120], wbra_sb[:, c, 2048:2120]))
    for n in range(2):
        pieces.append((bflat_in[:, n * 1024:(n + 1) * 1024], bflat_sb[:, n * 1024:(n + 1) * 1024]))
    n0 = len(stage_hist)
    stream_weight(pieces, staging=wstg45, nslots=3)
    ffn_ready = stage_hist[-1]

    woA = [0] * NT_OWN
    y1A = [0] * NT_OWN
    y1_dma = [0] * NT_OWN
    o_dma = [0] * NT_OWN
    x_dmaA = [0] * NT_OWN
    rsg_val = [0] * (NT_OWN // 4)

    for i in range(NT_OWN):
        slot = i % 2
        hi = i + 8

        @block.sync
        def _(sync, i=i, slot=slot, hi=hi):
            if i >= 2:
                sync.wait_ge(s_dve, y1A[i - 2])
            sync.wait_ge(d_sc, ofm_dma_val[i])
            sync.dma_start(out=ot_sb[:, slot, :], in_=scr_o[i]).then_inc(d_sc, 16)
            dma["sc"] += 16
            o_dma[i] = dma["sc"]
            sync.dma_start(out=xbuf[:, slot, :], in_=xh[hi * 128:(hi + 1) * 128, :]).then_inc(d_x, 16)
            dma["x"] += 16
            x_dmaA[i] = dma["x"]

        @block.tensor
        def _(tensor, i=i, slot=slot):
            tensor.wait_ge(d_sc, o_dma[i])
            if i >= 2:
                tensor.wait_ge(s_dve, y1A[i - 2])
            for c in range(16):
                if i == 0:
                    tensor.wait_ge(s_act, wo_vals[2 * c + 1])
                lhsT = ot_sb[:, slot, c * 128:(c + 1) * 128]
                for n in range(4):
                    ins = tensor.matmul(ps_wo[slot][:, n * 512:(n + 1) * 512], lhsT,
                                        wo_sb[:, c, n * 512:(n + 1) * 512],
                                        start=(c == 0), stop=(c == 15))
            woA[i] = PE.inc(ins)

        @block.vector
        def _(vector, i=i, slot=slot):
            vector.wait_ge(s_pe, woA[i])
            vector.wait_ge(d_x, x_dmaA[i])
            vector.scalar_tensor_tensor(y1b[:, slot, :], ps_wo[slot][:], 1.0,
                                        xbuf[:, slot, :], ALU.mult, ALU.add,
                                        accum_out=stA_sum[:, i:i + 1])
            vector.drain()
            ins = vector.scalar_tensor_tensor(xhat[:], y1b[:, slot, :], 1.0,
                                              y1b[:, slot, :], ALU.mult, ALU.mult,
                                              accum_out=stA_ss[:, i:i + 1])
            y1A[i] = DVE.inc(ins)

        @block.sync
        def _(sync, i=i, slot=slot):
            sync.wait_ge(s_dve, y1A[i])
            sync.dma_start(out=scr_y1[i], in_=y1b[:, slot, :]).then_inc(d_y1, 16)
            dma["y1"] += 16
            y1_dma[i] = dma["y1"]

        if i % 4 == 3:
            g = i // 4

            @block.vector
            def _(vector, g=g, i=i):
                vector.wait_ge(s_dve, y1A[i])
                vector.drain()
                vector.tensor_scalar(m2_[:, g * 4:g * 4 + 4], stA_sum[:, g * 4:g * 4 + 4],
                                     1.0 / E, None, ALU.mult)
                vector.drain()
                vector.tensor_tensor(msq2[:, g * 4:g * 4 + 4], m2_[:, g * 4:g * 4 + 4],
                                     m2_[:, g * 4:g * 4 + 4], ALU.mult)
                vector.drain()
                ins = vector.scalar_tensor_tensor(var2[:, g * 4:g * 4 + 4],
                                                  stA_ss[:, g * 4:g * 4 + 4], 1.0 / E,
                                                  msq2[:, g * 4:g * 4 + 4],
                                                  ALU.mult, ALU.subtract)
                DVE.inc(ins)

            @block.scalar
            def _(scalar, g=g):
                scalar.wait_ge(s_dve, DVE.n)
                scalar.activation(var2[:, g * 4:g * 4 + 4], var2[:, g * 4:g * 4 + 4],
                                  AF.Ln, bias=eps_ln[:])
                scalar.drain()
                ins = scalar.activation(rs2[:, g * 4:g * 4 + 4], var2[:, g * 4:g * 4 + 4],
                                        AF.Exp, scale=-0.5)
                rsg_val[g] = ACT.inc(ins)

    a_done_pe = woA[NT_OWN - 1]
    a_done_dve = y1A[NT_OWN - 1]
    a_done_dma = y1_dma[NT_OWN - 1]
    stA.close()

    stB = ExitStack()
    eB = stB.enter_context
    y1c = eB(nc.sbuf_tensor("y1c", [128, 3, E], F32))
    xT2 = eB(nc.sbuf_tensor("xT2", [128, 2, E], BF16))
    hpr2 = eB(nc.sbuf_tensor("hpr2", [128, 2, 64], BF16))
    lgha = eB(nc.sbuf_tensor("lgha", [128, 72], F32))
    lg = eB(nc.sbuf_tensor("lg", [128, 8], F32))
    gm1 = eB(nc.sbuf_tensor("gm1", [128, 1], F32))
    gnm1 = eB(nc.sbuf_tensor("gnm1", [128, 1], F32))
    gm2 = eB(nc.sbuf_tensor("gm2", [128, 1], F32))
    gtmp = eB(nc.sbuf_tensor("gtmp", [128, 8], F32))
    gwex = eB(nc.sbuf_tensor("gwex", [128, 8], F32))
    gsum = eB(nc.sbuf_tensor("gsum", [128, 1], F32))
    hT = eB(nc.sbuf_tensor("hT", [64, 128], BF16))
    ps_ffn = eB(nc.psum_tensor("ps_ffn", [128, E], F32))
    ps_trB = eB(nc.psum_tensor("ps_trB", [128, E], BF16))
    ps_ra = eB(nc.psum_tensor("ps_ra", [128, 72], F32))
    ps_hT = eB(nc.psum_tensor("ps_hT", [64, 128], BF16))

    ffn_done = [0] * NT_OWN
    out_dma = [0] * NT_OWN
    raB = [0] * NT_OWN
    ra_ev = [0] * NT_OWN
    normB = [0] * NT_OWN
    trB = [0] * NT_OWN
    xTB = [0] * NT_OWN
    mmB = [0] * NT_OWN
    gateB = [0] * NT_OWN
    hTpe = [0] * NT_OWN
    hTev = [0] * NT_OWN
    y1_dma2 = [0] * NT_OWN

    def emit_lagged(j):
        slot = j % 2
        slot3 = j % 3

        @block.tensor
        def _(tensor, j=j, slot=slot):
            tensor.wait_ge(s_dve, gateB[j])
            if j >= 1:
                tensor.wait_ge(s_dve, hTev[j - 1])
            ins = tensor.transpose(ps_hT[:], hpr2[:, slot, :], ident[:])
            hTpe[j] = PE.inc(ins)

        @block.vector
        def _(vector, j=j):
            vector.wait_ge(s_pe, hTpe[j])
            ins = vector.tensor_copy(hT[:], ps_hT[:])
            hTev[j] = DVE.inc(ins)

        @block.tensor
        def _(tensor, j=j, slot=slot):
            if j == 0:
                tensor.wait_ge(s_act, bb_ready)
            if j >= 1:
                tensor.wait_ge(s_dve, ffn_done[j - 1])
            for n in range(4):
                ins = tensor.matmul(ps_ffn[:, n * 512:(n + 1) * 512], onesc[:],
                                    bb_bf[:, n * 512:(n + 1) * 512],
                                    start=True, stop=False)
            for c in range(16):
                lhsT = xT2[:, slot, c * 128:(c + 1) * 128]
                for n in range(4):
                    ins = tensor.matmul(ps_ffn[:, n * 512:(n + 1) * 512], lhsT,
                                        wbra_sb[:, c, n * 512:(n + 1) * 512],
                                        start=False, stop=False)
            tensor.wait_ge(s_dve, hTev[j])
            for n in range(4):
                ins = tensor.matmul(ps_ffn[:, n * 512:(n + 1) * 512], hT[:],
                                    bflat_sb[:, n * 512:(n + 1) * 512],
                                    start=False, stop=True)
            mmB[j] = PE.inc(ins)

        @block.vector
        def _(vector, j=j, slot3=slot3):
            vector.wait_ge(s_pe, mmB[j])
            ins = vector.tensor_add(y1c[:, slot3, :], y1c[:, slot3, :], ps_ffn[:])
            ffn_done[j] = DVE.inc(ins)

        @block.gpsimd
        def _(gpsimd, j=j, slot3=slot3):
            gpsimd.wait_ge(s_dve, ffn_done[j])
            gpsimd.dma_start(out=out[j * 128:(j + 1) * 128, :], in_=y1c[:, slot3, :]).then_inc(d_out, 16)
            dma["out"] += 16
            out_dma[j] = dma["out"]

    for i in range(NT_OWN):
        slot = i % 2
        slot3 = i % 3

        @block.sync
        def _(sync, i=i, slot3=slot3):
            if i < 2:
                sync.wait_ge(s_pe, a_done_pe)
                sync.wait_ge(s_dve, a_done_dve)
                sync.wait_ge(d_y1, a_done_dma)
                sync.wait_ge(s_act, rsg_val[NT_OWN // 4 - 1])
            if i >= 3:
                sync.wait_ge(s_dve, ffn_done[i - 3])
                sync.wait_ge(d_out, out_dma[i - 3])
            sync.wait_ge(d_y1, y1_dma[i])
            sync.dma_start(out=y1c[:, slot3, :], in_=scr_y1[i]).then_inc(d_sc, 16)
            dma["sc"] += 16
            y1_dma2[i] = dma["sc"]

        @block.vector
        def _(vector, i=i, slot3=slot3):
            vector.wait_ge(d_sc, y1_dma2[i])
            vector.wait_ge(s_act, rsg_val[i // 4])
            if i >= 1:
                vector.wait_ge(s_pe, trB[i - 1])
            rsb = AP(tensor=rs2[:].tensor, offset=rs2[:].offset + i,
                     ap=[rs2[:].ap[0], [0, E]])
            ins = vector.scalar_tensor_tensor(xhat[:], y1c[:, slot3, :],
                                              m2_[:, i:i + 1], rsb,
                                              ALU.subtract, ALU.mult)
            normB[i] = DVE.inc(ins)

        @block.tensor
        def _(tensor, i=i):
            tensor.wait_ge(s_dve, normB[i])
            if i >= 1:
                tensor.wait_ge(s_dve, xTB[i - 1])
            for c in range(16):
                ins = tensor.transpose(ps_trB[:, c * 128:(c + 1) * 128],
                                       xhat[:, c * 128:(c + 1) * 128], ident[:])
            trB[i] = PE.inc(ins)

        @block.vector
        def _(vector, i=i, slot=slot):
            vector.wait_ge(s_pe, trB[i])
            if i >= 2:
                vector.wait_ge(s_pe, mmB[i - 2])
            ins = vector.tensor_copy(xT2[:, slot, :], ps_trB[:])
            xTB[i] = DVE.inc(ins)

        @block.tensor
        def _(tensor, i=i, slot=slot):
            tensor.wait_ge(s_dve, xTB[i])
            if i == 0:
                tensor.wait_ge(s_act, ffn_ready)
            if i >= 1:
                tensor.wait_ge(s_dve, ra_ev[i - 1])
            for c in range(16):
                ins = tensor.matmul(ps_ra[:], xT2[:, slot, c * 128:(c + 1) * 128],
                                    wbra_sb[:, c, 2048:2120],
                                    start=(c == 0), stop=(c == 15))
            raB[i] = PE.inc(ins)

        @block.vector
        def _(vector, i=i):
            vector.wait_ge(s_pe, raB[i])
            ins = vector.tensor_copy(lgha[:], ps_ra[:])
            ra_ev[i] = DVE.inc(ins)
            vector.drain()
            vector.tensor_copy(lg[:], lgha[:, 0:8])
            vector.drain()
            vector.tensor_reduce(gm1[:], lg[:], AX.X, ALU.max)
            vector.drain()
            vector.tensor_scalar(gtmp[:], lg[:], gm1[:], -1e30, ALU.is_ge, ALU.mult)
            vector.drain()
            vector.tensor_add(gtmp[:], gtmp[:], lg[:])
            vector.drain()
            vector.tensor_reduce(gm2[:], gtmp[:], AX.X, ALU.max)
            vector.drain()
            ins = vector.tensor_scalar(gnm1[:], gm1[:], -1.0, None, ALU.mult)
            DVE.inc(ins)

        @block.scalar
        def _(scalar):
            scalar.wait_ge(s_dve, DVE.n)
            ins = scalar.activation(gwex[:], lg[:], AF.Exp, bias=gnm1[:])
            ACT.inc(ins)

        @block.vector
        def _(vector, i=i, slot=slot):
            vector.wait_ge(s_act, ACT.n)
            vector.tensor_scalar(gtmp[:], lg[:], gm2[:], None, ALU.is_ge)
            vector.drain()
            vector.tensor_mul(gwex[:], gwex[:], gtmp[:])
            vector.drain()
            vector.tensor_reduce(gsum[:], gwex[:], AX.X, ALU.add)
            vector.drain()
            vector.reciprocal(gsum[:], gsum[:])
            vector.drain()
            vector.tensor_scalar(gsum[:], gsum[:], SCALE_MOE, None, ALU.mult)
            vector.drain()
            vector.tensor_scalar(gwex[:], gwex[:], gsum[:], None, ALU.mult)
            vector.drain()
            if i >= 2:
                vector.wait_ge(s_pe, hTpe[i - 2])
            wb = AP(tensor=gwex[:].tensor, offset=gwex[:].offset,
                    ap=[gwex[:].ap[0], [1, 8], [0, 8]])
            ins = vector.tensor_tensor(hpr2[:, slot, :].rearrange("p (e r) -> p e r", e=8),
                                       lgha[:, 8:72].rearrange("p (e r) -> p e r", e=8),
                                       wb, ALU.mult)
            gateB[i] = DVE.inc(ins)

        if i >= 1:
            emit_lagged(i - 1)
    emit_lagged(NT_OWN - 1)

    @block.sync
    def _(sync):
        sync.wait_ge(d_out, dma["out"])

    stB.close()
    stW.close()
    st_p45.close()
    st_top.close()
    return nc


_CACHE = {}


def make_in_maps(inputs):
    x = _f32(inputs["L0_attn_hidden_states"])
    wq = _f32(inputs["L0_attn_wq"])
    wk = _f32(inputs["L0_attn_wk"])
    wv = _f32(inputs["L0_attn_wv"])
    wo = _f32(inputs["L0_attn_wo"])
    qscale = _f32(inputs["L0_attn_query_scale"])
    kscale = _f32(inputs["L0_attn_key_scale"])
    W_base = _f32(inputs["L0_ffn_W_base"])
    b_base = _f32(inputs["L0_ffn_b_base"])
    W_router = _f32(inputs["L0_ffn_W_router"])
    A_w = _f32(inputs["L0_ffn_A_w"])
    B_w = _f32(inputs["L0_ffn_B_w"])

    wbra = _f32(np.concatenate(
        [W_base.T, W_router.T, A_w.reshape(NEXP * RANK, E).T], axis=1))
    bflat = _f32(np.transpose(B_w, (0, 2, 1)).reshape(NEXP * RANK, E))
    b_bcast = _f32(b_base[None, :])
    qs_rep = _f32(np.tile(qscale, (128, H)))
    ks_rep = _f32(np.tile(kscale, (128, KVH)))
    inv_freq = 1.0 / (10000.0 ** (np.arange(0, D // 2, dtype=np.float64) / (D // 2)))

    in_maps = []
    for core in range(NCORE):
        b = core // 4
        q4 = core % 4
        own0 = q4 * TOWN
        lo, hi = own0 - WIN, own0 + TOWN + WIN
        lo_c, hi_c = max(lo, 0), min(hi, S)
        xs = x[b, lo_c:hi_c, :]
        pad_l, pad_r = lo_c - lo, hi - hi_c
        if pad_l or pad_r:
            xs = np.concatenate(
                [np.repeat(xs[:1], pad_l, 0), xs, np.repeat(xs[-1:], pad_r, 0)], 0)
        pos = np.clip(np.arange(lo, hi, dtype=np.float64), 0, S - 1)
        ang = np.outer(pos, inv_freq)
        cos_t = _f32(np.concatenate([np.cos(ang), np.cos(ang)], 1))
        sin_t = _f32(np.concatenate([-np.sin(ang), np.sin(ang)], 1))
        vb = np.zeros((NT_OWN, NCH), dtype=np.float64)
        for qt in range(NT_OWN):
            for c in range(NCH):
                kb = own0 - WIN + 128 * (qt + c)
                vb[qt, c] = 1.0 if (0 <= kb <= S - 128) else 0.0
        vb = _f32(np.broadcast_to(vb.reshape(1, -1), (128, NT_OWN * NCH)))
        in_maps.append({
            "xh": _f32(xs), "cos_t": cos_t, "sin_t": sin_t, "vb": vb,
            "wq": wq, "wk": wk, "wv": wv, "wo": wo,
            "wbra": wbra, "bflat": bflat, "b_bcast": b_bcast,
            "qscale_rep": qs_rep, "kscale_rep": ks_rep,
        })
    return in_maps


def kernel(**inputs):
    in_maps = make_in_maps(inputs)
    if "nc" not in _CACHE:
        _CACHE["nc"] = build_program()
    nc = _CACHE["nc"]
    res = run_bass_kernel_spmd(nc, in_maps, core_ids=list(range(NCORE)))
    outp = np.empty((B, S, E), dtype=np.float32)
    for core in range(NCORE):
        b = core // 4
        q4 = core % 4
        outp[b, q4 * TOWN:(q4 + 1) * TOWN, :] = res.results[core]["out"]
    return outp


# version 56
# speedup vs baseline: 1.0216x; 1.0216x over previous
import numpy as np
import ml_dtypes
from contextlib import ExitStack

import concourse.bass as bass
import concourse.mybir as mybir
from concourse.bass_utils import run_bass_kernel_spmd

F32 = mybir.dt.float32
BF16 = mybir.dt.bfloat16
AF = mybir.ActivationFunctionType
ALU = mybir.AluOpType
AX = mybir.AxisListType
AP = bass.AP

B, S, E = 2, 8192, 2048
H, KVH, D = 16, 4, 128
WIN = 1024
CAP = 50.0
EPS_ATT = 1e-6
EPS_LN = 1e-5
NCORE = 8
TOWN = 2048
HALO = 4096
NT_OWN = TOWN // 128
NT_HALO = HALO // 128
NCH = 17
NPAIR = 9
NEXP, RANK = 8, 8
SCALE_MOE = 2.0


def _f32(x):
    return np.ascontiguousarray(x, dtype=np.float32)


def _mask_kind(qt, c):
    tri = "L" if c == 0 else ("R" if c == 16 else None)
    vb = (qt + c <= 7) or (qt + c >= 24)
    return tri, vb


class Ctr:
    def __init__(self, sem):
        self.sem = sem
        self.n = 0

    def inc(self, ins):
        ins.then_inc(self.sem, 1)
        self.n += 1
        return self.n


def build_program(upto="full", debug=False):
    nc = bass.Bass("TRN2", target_bir_lowering=False)

    xh = nc.dram_tensor("xh", [HALO, E], F32, kind="ExternalInput").ap()
    cos_in = nc.dram_tensor("cos_t", [HALO, D], F32, kind="ExternalInput").ap()
    sin_in = nc.dram_tensor("sin_t", [HALO, D], F32, kind="ExternalInput").ap()
    vb_in = nc.dram_tensor("vb", [128, NT_OWN * NCH], F32, kind="ExternalInput").ap()
    wq_in = nc.dram_tensor("wq", [E, H * D], F32, kind="ExternalInput").ap()
    wk_in = nc.dram_tensor("wk", [E, KVH * D], F32, kind="ExternalInput").ap()
    wv_in = nc.dram_tensor("wv", [E, KVH * D], F32, kind="ExternalInput").ap()
    wo_in = nc.dram_tensor("wo", [H * D, E], F32, kind="ExternalInput").ap()
    wbra_in = nc.dram_tensor("wbra", [E, 2120], F32, kind="ExternalInput").ap()
    bflat_in = nc.dram_tensor("bflat", [NEXP * RANK, E], F32, kind="ExternalInput").ap()
    bb_in = nc.dram_tensor("b_bcast", [1, E], F32, kind="ExternalInput").ap()
    qs_in = nc.dram_tensor("qscale_rep", [128, H * D], F32, kind="ExternalInput").ap()
    ks_in = nc.dram_tensor("kscale_rep", [128, KVH * D], F32, kind="ExternalInput").ap()

    scr_o = nc.dram_tensor("scr_o", [NT_OWN, 128, H * 128], BF16, kind="Internal").ap()
    out = nc.dram_tensor("out", [TOWN, E], F32, kind="ExternalOutput").ap()

    ident_np = np.eye(128, dtype=np.float32).astype(ml_dtypes.bfloat16)
    jj_ = np.arange(128)[:, None]
    ii_ = np.arange(128)[None, :]
    maskL_np = np.tile((jj_ >= ii_).astype(np.float32), (1, 4)).astype(ml_dtypes.bfloat16)
    maskR_np = np.tile((jj_ <= ii_).astype(np.float32), (1, 4)).astype(ml_dtypes.bfloat16)
    ident_h = nc.inline_tensor(ident_np, name="c_ident").ap()
    ones_h = nc.inline_tensor(np.ones((1, 128), dtype=np.float32).astype(ml_dtypes.bfloat16),
                              name="c_ones").ap()
    maskL_h = nc.inline_tensor(maskL_np, name="c_maskL").ap()
    maskR_h = nc.inline_tensor(maskR_np, name="c_maskR").ap()

    dbg = {}
    if debug:
        dbg["K"] = nc.dram_tensor("dbg_K", [128, KVH, HALO], BF16, kind="ExternalOutput").ap()
        dbg["V"] = nc.dram_tensor("dbg_V", [128, NT_HALO, KVH, 129], BF16, kind="ExternalOutput").ap()
        dbg["Q"] = nc.dram_tensor("dbg_Q", [128, H, TOWN], BF16, kind="ExternalOutput").ap()
        dbg["O"] = nc.dram_tensor("dbg_O", [NT_OWN, 128, H * 128], BF16, kind="ExternalOutput").ap()
        dbg["XH"] = nc.dram_tensor("dbg_XH", [128, E], BF16, kind="ExternalOutput").ap()
        dbg["QR"] = nc.dram_tensor("dbg_QR", [128, 1024], F32, kind="ExternalOutput").ap()

    st_top = ExitStack()
    st_kvq = ExitStack()
    st_p12 = ExitStack()
    st_p1 = ExitStack()
    st_p2 = ExitStack()
    st_p3 = ExitStack()
    st_p45 = ExitStack()

    en = st_top.enter_context
    ident = en(nc.sbuf_tensor("ident", [128, 128], BF16))
    rs_all = en(nc.sbuf_tensor("rs_all", [128, NT_HALO], F32))
    m_all = en(nc.sbuf_tensor("m_all", [128, NT_HALO], F32))
    eps_ln = en(nc.sbuf_tensor("eps_ln", [128, 1], F32))
    xbuf = en(nc.sbuf_tensor("xbuf", [128, 2, E], F32))
    xhat = en(nc.sbuf_tensor("xhat", [128, E], BF16))

    ek = st_kvq.enter_context
    K_sb = ek(nc.sbuf_tensor("K_sb", [128, KVH, HALO], BF16))
    Vp_sb = ek(nc.sbuf_tensor("Vp_sb", [128, NT_HALO, KVH, 129], BF16))
    Qfm = ek(nc.sbuf_tensor("Qfm", [128, H, TOWN], BF16))

    e12 = st_p12.enter_context
    wstg = e12(nc.sbuf_tensor("wstg", [128, 3, 512], F32))
    xT = e12(nc.sbuf_tensor("xT", [128, E], BF16))
    tbl = e12(nc.sbuf_tensor("tbl", [128, 3, 2, D], F32))
    rp_rot = e12(nc.sbuf_tensor("rp_rot", [128, 2, 512], F32))
    rp_t1 = e12(nc.sbuf_tensor("rp_t1", [128, 2, 512], F32))
    rp_ss = e12(nc.sbuf_tensor("rp_ss", [128, 2, 4], F32))
    rp_f = e12(nc.sbuf_tensor("rp_f", [128, 2, 4], F32))
    qk_bf = e12(nc.sbuf_tensor("qk_bf", [128, 2, 512], BF16))
    ps_tr = [e12(nc.psum_tensor(f"ps_tr{k}", [128, E], BF16)) for k in range(2)]

    d_x = en(nc.semaphore("d_x"))
    d_tb = en(nc.semaphore("d_tb"))
    d_w = en(nc.semaphore("d_w"))
    d_w2 = en(nc.semaphore("d_w2"))
    d_y1 = en(nc.semaphore("d_y1"))
    d_sc = en(nc.semaphore("d_sc"))
    d_out = en(nc.semaphore("d_out"))
    d_c = en(nc.semaphore("d_c"))
    s_pe = en(nc.semaphore("s_pe"))
    s_act = en(nc.semaphore("s_act"))
    s_dve = en(nc.semaphore("s_dve"))
    s_pool = en(nc.semaphore("s_pool"))

    block = en(nc.Block())

    PE = Ctr(s_pe)
    ACT = Ctr(s_act)
    DVE = Ctr(s_dve)
    POOL = Ctr(s_pool)
    dma = {"x": 0, "tb": 0, "w": 0, "w2": 0, "sc": 0, "out": 0, "c": 0, "y1": 0}

    def fence():
        marks = [(s_pe, PE.n), (s_act, ACT.n), (s_dve, DVE.n), (s_pool, POOL.n)]
        dmas = [(d_x, dma["x"]), (d_tb, dma["tb"]), (d_w, dma["w"]),
                (d_w2, dma["w2"]), (d_sc, dma["sc"]), (d_out, dma["out"]),
                (d_c, dma["c"]), (d_y1, dma["y1"])]
        for name in ("tensor", "scalar", "vector", "gpsimd", "sync"):
            @getattr(block, name)
            def _(eng):
                for sem, v in marks + dmas:
                    if v:
                        eng.wait_ge(sem, v)

    def waits(eng, guards):
        for sem, v in guards:
            if v:
                eng.wait_ge(sem, v)

    stage_hist = []

    def stream_weight(pieces, staging=None, nslots=3):
        stg = staging if staging is not None else wstg
        for src, dst in pieces:
            slot = len(stage_hist) % nslots
            qidx = len(stage_hist) % 2
            qname = ("gpsimd", "scalar")[qidx]
            sem, key = ((d_w, "w"), (d_w2, "w2"))[qidx]
            npart = src.shape[0]
            ncol = src.shape[-1]

            @getattr(block, qname)
            def _(eng, slot=slot, src=src, ncol=ncol, npart=npart, sem=sem, key=key):
                if len(stage_hist) >= nslots - 1:
                    eng.wait_ge(s_act, stage_hist[-(nslots - 1)])
                eng.dma_start(out=stg[0:npart, slot, 0:ncol], in_=src).then_inc(sem, 16)
                dma[key] += 16
            wval = dma[key]

            @block.scalar
            def _(scalar, slot=slot, dst=dst, ncol=ncol, npart=npart, sem=sem, wval=wval):
                scalar.wait_ge(sem, wval)
                ins = scalar.activation(dst, stg[0:npart, slot, 0:ncol], AF.Copy)
                stage_hist.append(ACT.inc(ins))
        return stage_hist[-1]

    e1 = st_p1.enter_context
    wkv = e1(nc.sbuf_tensor("wkv", [128, 16, 1024], BF16))
    ks_sb = e1(nc.sbuf_tensor("ks_sb", [128, KVH * D], F32))
    ps_kv = [e1(nc.psum_tensor(f"ps_kv{k}", [128, 1024], F32)) for k in range(2)]

    @block.sync
    def _(sync):
        for dst, src in [(ident, ident_h), (ks_sb, ks_in)]:
            sync.dma_start(out=dst[:], in_=src).then_inc(d_c, 16)
            dma["c"] += 16
    const_dma = dma["c"]

    wk_r = wk_in.rearrange("(c p) f -> c p f", p=128)
    wv_r = wv_in.rearrange("(c p) f -> c p f", p=128)
    pieces = []
    for c in range(16):
        pieces.append((wk_r[c], wkv[:, c, 0:512]))
        pieces.append((wv_r[c], wkv[:, c, 512:1024]))
    wkv_ready = stream_weight(pieces)

    @block.vector
    def _(vector):
        ins = vector.memset(Vp_sb[:, :, :, 128:129], 1.0)
        DVE.inc(ins)
    vp_ones = DVE.n

    def tbl_aps(tslot):
        cosb = AP(tensor=tbl[:].tensor, offset=tbl[:].offset + tslot * 2 * D,
                  ap=[tbl[:].ap[0], [0, 4], [1, D]])
        sinb = AP(tensor=tbl[:].tensor, offset=tbl[:].offset + tslot * 2 * D + D,
                  ap=[tbl[:].ap[0], [0, 4], [1, D]])
        return cosb, sinb

    def v4(ap2d):
        return ap2d.rearrange("p (h d) -> p h d", h=4, d=D)

    def rope_ops(kp, rslot, tslot):
        kps = kp.rearrange("p (h s d) -> p h s d", h=4, s=2, d=64)
        rot_in = AP(tensor=kps.tensor, offset=kps.offset + 64,
                    ap=[kps.ap[0], kps.ap[1], [-64, 2], [1, 64]])
        rot = rp_rot[:, rslot, :]
        t1 = rp_t1[:, rslot, :]
        cosb, sinb = tbl_aps(tslot)
        yield lambda v: v.tensor_copy(rot.rearrange("p (h s d) -> p h s d", h=4, s=2, d=64), rot_in)
        yield lambda v: v.tensor_tensor(v4(t1), v4(kp), cosb, ALU.mult)
        yield lambda v: v.tensor_tensor(v4(rot), v4(rot), sinb, ALU.mult)
        yield lambda v: v.tensor_add(t1, t1, rot)
        yield lambda v: v.tensor_tensor(rot, t1, t1, ALU.mult)
        yield lambda v: v.tensor_reduce(rp_ss[:, rslot, :], v4(rot), AX.X, ALU.add)

    def apply_ops(rslot, scale_ap, qk_dst):
        f = rp_f[:, rslot, :]
        t1 = rp_t1[:, rslot, :]
        rot = rp_rot[:, rslot, :]
        fb = AP(tensor=f.tensor, offset=f.offset, ap=[f.ap[0], [1, 4], [0, D]])
        yield lambda v: v.tensor_scalar(f, f, EPS_ATT, None, ALU.add)
        yield lambda v: v.reciprocal(f, f)
        yield lambda v: v.tensor_tensor(v4(rot), v4(t1), fb, ALU.mult)
        yield lambda v: v.tensor_tensor(qk_dst, rot, scale_ap, ALU.mult)

    def emit_rope1(vector, kp, rslot, tslot):
        ops = list(rope_ops(kp, rslot, tslot))
        ops[0](vector)
        ops[1](vector)
        vector.drain()
        ops[2](vector)
        vector.drain()
        ops[3](vector)
        vector.drain()
        ops[4](vector)
        vector.drain()
        return ops[5](vector)

    def emit_apply(vector, rslot, scale_ap):
        ops = list(apply_ops(rslot, scale_ap, qk_bf[:, rslot, :]))
        ops[0](vector)
        vector.drain()
        ops[1](vector)
        vector.drain()
        ops[2](vector)
        vector.drain()
        return ops[3](vector)

    @block.vector
    def _(vector):
        ins = vector.memset(eps_ln[:], EPS_LN)
        DVE.inc(ins)

    pp_sum = e1(nc.sbuf_tensor("pp_sum", [128, NT_HALO], F32))
    pp_ss = e1(nc.sbuf_tensor("pp_ss", [128, NT_HALO], F32))
    pp_dma = [0] * NT_HALO
    pp_val = [0] * NT_HALO
    for i in range(NT_HALO):
        slot = i % 2

        @block.sync
        def _(sync, i=i, slot=slot):
            if i >= 2:
                sync.wait_ge(s_dve, pp_val[i - 2])
            sync.dma_start(out=xbuf[:, slot, :], in_=xh[i * 128:(i + 1) * 128, :]).then_inc(d_x, 16)
            dma["x"] += 16
            pp_dma[i] = dma["x"]

        @block.vector
        def _(vector, i=i, slot=slot):
            vector.wait_ge(d_x, pp_dma[i])
            vector.tensor_scalar(xhat[:], xbuf[:, slot, :], 1.0, 0.0, ALU.mult,
                                 ALU.add, accum_out=pp_sum[:, i:i + 1])
            ins = vector.scalar_tensor_tensor(xhat[:], xbuf[:, slot, :], 1.0,
                                              xbuf[:, slot, :], ALU.mult, ALU.mult,
                                              accum_out=pp_ss[:, i:i + 1])
            pp_val[i] = DVE.inc(ins)

    @block.vector
    def _(vector):
        vector.wait_ge(s_dve, pp_val[NT_HALO - 1])
        vector.drain()
        vector.tensor_scalar(m_all[:], pp_sum[:], 1.0 / E, None, ALU.mult)
        vector.tensor_scalar(pp_ss[:], pp_ss[:], 1.0 / E, None, ALU.mult)
        vector.drain()
        vector.tensor_tensor(pp_sum[:], m_all[:], m_all[:], ALU.mult)
        vector.drain()
        ins = vector.tensor_sub(pp_sum[:], pp_ss[:], pp_sum[:])
        DVE.inc(ins)

    @block.scalar
    def _(scalar):
        scalar.wait_ge(s_dve, DVE.n)
        scalar.activation(pp_sum[:], pp_sum[:], AF.Ln, bias=eps_ln[:])
        scalar.drain()
        ins = scalar.activation(rs_all[:], pp_sum[:], AF.Exp, scale=-0.5)
        ACT.inc(ins)
    ln1_ready = ACT.n

    x_dma = [0] * NT_HALO
    tb_dma = [0] * NT_HALO
    a_val = [0] * (NT_HALO + 4)
    b_val = [0] * (NT_HALO + 4)
    c_val = [0] * (NT_HALO + 4)
    ktr_val = [0] * NT_HALO
    trpe_val = [0] * NT_HALO
    kvmm_val = [0] * NT_HALO
    kn_val = [0] * NT_HALO
    kv_done_last = 0

    NIT1 = NT_HALO + 3
    for it in range(NIT1):
        i_dma = it
        i_nrm = it - 1
        i_rp = it - 2
        i_fin = it - 3

        if i_dma < NT_HALO:
            @block.sync
            def _(sync, i=i_dma):
                slot = i % 2
                if i < 2:
                    sync.wait_ge(s_dve, pp_val[NT_HALO - 2 + i])
                if i >= 2:
                    sync.wait_ge(s_dve, a_val[i - 1])
                if i >= 3:
                    sync.wait_ge(s_dve, c_val[i - 1])
                sync.dma_start(out=xbuf[:, slot, :], in_=xh[i * 128:(i + 1) * 128, :]).then_inc(d_x, 16)
                dma["x"] += 16
                x_dma[i] = dma["x"]
                sync.dma_start(out=tbl[:, i % 3, 0, :], in_=cos_in[i * 128:(i + 1) * 128, :]).then_inc(d_tb, 16)
                sync.dma_start(out=tbl[:, i % 3, 1, :], in_=sin_in[i * 128:(i + 1) * 128, :]).then_inc(d_tb, 16)
                dma["tb"] += 32
                tb_dma[i] = dma["tb"]

        @block.vector
        def _(vector, it=it, i_fin=i_fin, i_nrm=i_nrm):
            ops = []
            if 0 <= i_fin < NT_HALO:
                vector.wait_ge(s_act, kn_val[i_fin])
                if i_fin >= 2:
                    vector.wait_ge(s_pe, ktr_val[i_fin - 2])
                rslot = i_fin % 2
                aps = list(apply_ops(rslot, ks_sb[:], qk_bf[:, rslot, :]))
                pslot = i_fin % 2
                vdst = AP(tensor=Vp_sb[:].tensor,
                          offset=Vp_sb[:].offset + i_fin * (KVH * 129),
                          ap=[Vp_sb[:].ap[0], [129, 4], [1, 128]])
                vcopy = lambda v: v.tensor_copy(
                    vdst, ps_kv[pslot][:, 512:1024].rearrange("p (h d) -> p h d", h=4, d=D))
                ops = [aps[0], vcopy, aps[1], aps[2], aps[3]]
            nrm = []
            if 0 <= i_nrm < NT_HALO:
                vector.wait_ge(d_x, x_dma[i_nrm])
                vector.wait_ge(s_act, ln1_ready)
                if i_nrm >= 1:
                    vector.wait_ge(s_pe, trpe_val[i_nrm - 1])
                slot = i_nrm % 2
                nrm = [
                    lambda v, slot=slot, i=i_nrm: v.tensor_scalar(
                        xbuf[:, slot, :], xbuf[:, slot, :], m_all[:, i:i + 1], None, ALU.subtract),
                    lambda v, slot=slot, i=i_nrm: v.tensor_scalar(
                        xhat[:], xbuf[:, slot, :], rs_all[:, i:i + 1], None, ALU.mult),
                ]
            if ops and nrm:
                seq = [ops[0], ops[1], ops[2], nrm[0], ops[3], nrm[1], ops[4]]
            elif ops:
                seq = [ops[0], ops[1], ops[2], None, ops[3], None, ops[4]]
            else:
                seq = [nrm[0], None, nrm[1]] if nrm else []
            ins = None
            for f in seq:
                if f is None:
                    vector.drain()
                else:
                    ins = f(vector)
            if ins is not None:
                a_val[it] = DVE.inc(ins)
            else:
                a_val[it] = DVE.n

        if 0 <= i_fin < NT_HALO:
            @block.tensor
            def _(tensor, it=it, i_fin=i_fin):
                tensor.wait_ge(s_dve, a_val[it])
                rslot = i_fin % 2
                for h in range(4):
                    ins = tensor.transpose(ps_tr[rslot][:, h * 128:(h + 1) * 128],
                                           qk_bf[:, rslot, h * 128:(h + 1) * 128], ident[:])
                ktr_val[i_fin] = PE.inc(ins)

            @block.vector
            def _(vector, it=it, i_fin=i_fin):
                vector.wait_ge(s_pe, ktr_val[i_fin])
                rslot = i_fin % 2
                kdst = AP(tensor=K_sb[:].tensor, offset=K_sb[:].offset + i_fin * 128,
                          ap=[K_sb[:].ap[0], [HALO, 4], [1, 128]])
                ins = vector.tensor_copy(
                    kdst, ps_tr[rslot][:, 0:512].rearrange("p (h d) -> p h d", h=4, d=128))
                b_val[it] = DVE.inc(ins)
        else:
            b_val[it] = DVE.n

        if 0 <= i_nrm < NT_HALO:
            @block.tensor
            def _(tensor, it=it, i_nrm=i_nrm):
                tensor.wait_ge(s_dve, a_val[it])
                tensor.wait_ge(s_dve, b_val[it])
                slot = i_nrm % 2
                for c in range(16):
                    ins = tensor.transpose(ps_tr[slot][:, c * 128:(c + 1) * 128],
                                           xhat[:, c * 128:(c + 1) * 128], ident[:])
                trpe_val[i_nrm] = PE.inc(ins)

        @block.vector
        def _(vector, it=it, i_rp=i_rp, i_nrm=i_nrm):
            ins = None
            rp = []
            if 0 <= i_rp < NT_HALO:
                vector.wait_ge(s_pe, kvmm_val[i_rp])
                vector.wait_ge(d_tb, tb_dma[i_rp])
                rp = list(rope_ops(ps_kv[i_rp % 2][:, 0:512], i_rp % 2, i_rp % 3))
                for f in rp[:3]:
                    ins = f(vector)
            if 0 <= i_nrm < NT_HALO:
                vector.wait_ge(s_pe, trpe_val[i_nrm])
                if i_nrm >= 1:
                    vector.wait_ge(s_pe, kvmm_val[i_nrm - 1])
                ins = vector.tensor_copy(xT[:], ps_tr[i_nrm % 2][:])
            elif rp:
                vector.drain()
            if rp:
                ins = rp[3](vector)
                vector.drain()
                ins = rp[4](vector)
                vector.drain()
                ins = rp[5](vector)
            if ins is not None:
                c_val[it] = DVE.inc(ins)
            else:
                c_val[it] = DVE.n

        if 0 <= i_nrm < NT_HALO:
            @block.tensor
            def _(tensor, it=it, i_nrm=i_nrm):
                tensor.wait_ge(s_dve, c_val[it])
                if i_nrm == 0:
                    tensor.wait_ge(s_act, wkv_ready)
                slot = i_nrm % 2
                for c in range(16):
                    lhsT = xT[:, c * 128:(c + 1) * 128]
                    tensor.matmul(ps_kv[slot][:, 0:512], lhsT, wkv[:, c, 0:512],
                                  start=(c == 0), stop=(c == 15))
                    ins = tensor.matmul(ps_kv[slot][:, 512:1024], lhsT, wkv[:, c, 512:1024],
                                        start=(c == 0), stop=(c == 15))
                kvmm_val[i_nrm] = PE.inc(ins)

        if 0 <= i_rp < NT_HALO:
            @block.scalar
            def _(scalar, it=it, i_rp=i_rp):
                scalar.wait_ge(s_dve, c_val[it])
                rslot = i_rp % 2
                scalar.activation(rp_f[:, rslot, :], rp_ss[:, rslot, :], AF.Ln)
                scalar.drain()
                ins = scalar.activation(rp_f[:, rslot, :], rp_f[:, rslot, :],
                                        AF.Exp, scale=0.5)
                kn_val[i_rp] = ACT.inc(ins)
    kv_done_last = b_val[NIT1 - 1]

    fence()
    st_p1.close()

    if debug:
        @block.sync
        def _(sync):
            sync.dma_start(out=dbg["K"], in_=K_sb[:]).then_inc(d_c, 16)
            sync.dma_start(out=dbg["V"], in_=Vp_sb[:]).then_inc(d_c, 16)
            dma["c"] += 32
            sync.wait_ge(d_c, dma["c"])
    if upto == "p1":
        st_p12.close()
        st_kvq.close()
        st_top.close()
        return nc

    fence()
    e2 = st_p2.enter_context
    wqh = e2(nc.sbuf_tensor("wqh", [128, 16, 1024], BF16))
    qs_sb = e2(nc.sbuf_tensor("qs_sb", [128, 2, 512], F32))
    ps_q = [e2(nc.psum_tensor(f"ps_q{k}", [128, 1024], F32)) for k in range(2)]
    wq_r = wq_in.rearrange("(c p) f -> c p f", p=128)

    for g in range(2):
        if g == 1:
            fence()
        pieces = []
        for c in range(16):
            for hcol in range(2):
                pieces.append((wq_r[c][:, g * 1024 + hcol * 512:g * 1024 + (hcol + 1) * 512],
                               wqh[:, c, hcol * 512:(hcol + 1) * 512]))
        n0 = len(stage_hist)
        stream_weight(pieces)
        wq_vals = stage_hist[n0:]

        @block.sync
        def _(sync, g=g):
            for sg in range(2):
                sync.dma_start(out=qs_sb[:, sg, :],
                               in_=qs_in[:, (g * 2 + sg) * 512:(g * 2 + sg + 1) * 512]).then_inc(d_c, 16)
                dma["c"] += 16
        qs_dma = dma["c"]

        x_dma2 = [0] * NT_OWN
        tb_dma2 = [0] * NT_OWN
        a2 = [0] * (NT_OWN + 4)
        b2 = [0] * (NT_OWN + 4)
        c2 = [0] * (NT_OWN + 4)
        xv = [0] * (NT_OWN + 4)
        qtr2 = [0] * NT_OWN
        trpe2 = [0] * NT_OWN
        qmm2 = [0] * NT_OWN
        kn2 = [0] * NT_OWN

        NIT2 = NT_OWN + 3
        for it in range(NIT2):
            i_dma = it
            i_nrm = it - 1
            i_rp = it - 2
            i_fin = it - 3

            if i_dma < NT_OWN:
                @block.sync
                def _(sync, i=i_dma, g=g):
                    slot = i % 2
                    hi = i + 8
                    if i >= 2:
                        sync.wait_ge(s_dve, a2[i - 1])
                    if i >= 3:
                        sync.wait_ge(s_dve, c2[i - 1])
                    sync.dma_start(out=xbuf[:, slot, :], in_=xh[hi * 128:(hi + 1) * 128, :]).then_inc(d_x, 16)
                    dma["x"] += 16
                    x_dma2[i] = dma["x"]
                    sync.dma_start(out=tbl[:, i % 3, 0, :], in_=cos_in[hi * 128:(hi + 1) * 128, :]).then_inc(d_tb, 16)
                    sync.dma_start(out=tbl[:, i % 3, 1, :], in_=sin_in[hi * 128:(hi + 1) * 128, :]).then_inc(d_tb, 16)
                    dma["tb"] += 32
                    tb_dma2[i] = dma["tb"]

            @block.vector
            def _(vector, it=it, i_fin=i_fin, i_nrm=i_nrm, g=g):
                ap0 = ap1 = None
                if 0 <= i_fin < NT_OWN:
                    vector.wait_ge(s_act, kn2[i_fin])
                    vector.wait_ge(d_c, qs_dma)
                    if i_fin >= 1:
                        vector.wait_ge(s_pe, qtr2[i_fin - 1])
                    ap0 = list(apply_ops(0, qs_sb[:, 0, :], qk_bf[:, 0, :]))
                    ap1 = list(apply_ops(1, qs_sb[:, 1, :], qk_bf[:, 1, :]))
                nrm = []
                if 0 <= i_nrm < NT_OWN:
                    vector.wait_ge(d_x, x_dma2[i_nrm])
                    if i_nrm >= 1:
                        vector.wait_ge(s_pe, trpe2[i_nrm - 1])
                    slot = i_nrm % 2
                    hi = i_nrm + 8
                    nrm = [
                        lambda v, slot=slot, hi=hi: v.tensor_scalar(
                            xbuf[:, slot, :], xbuf[:, slot, :], m_all[:, hi:hi + 1], None, ALU.subtract),
                        lambda v, slot=slot, hi=hi: v.tensor_scalar(
                            xhat[:], xbuf[:, slot, :], rs_all[:, hi:hi + 1], None, ALU.mult),
                    ]
                ins = None
                if ap0 and nrm:
                    seq = [ap0[0], ap1[0], nrm[0], ap0[1], ap1[1], nrm[1],
                           ap0[2], ap1[2], ap0[3], ap1[3]]
                elif ap0:
                    seq = [ap0[0], ap1[0], None, ap0[1], ap1[1], None,
                           ap0[2], ap1[2], ap0[3], ap1[3]]
                else:
                    seq = [nrm[0], None, nrm[1]] if nrm else []
                for f in seq:
                    if f is None:
                        vector.drain()
                    else:
                        ins = f(vector)
                a2[it] = DVE.inc(ins) if ins is not None else DVE.n

            if 0 <= i_fin < NT_OWN:
                @block.tensor
                def _(tensor, it=it, i_fin=i_fin):
                    tensor.wait_ge(s_dve, a2[it])
                    if it >= 1:
                        tensor.wait_ge(s_act, b2[it - 1])
                    for sg in range(2):
                        for h in range(4):
                            ins = tensor.transpose(
                                ps_tr[0][:, (sg * 4 + h) * 128:(sg * 4 + h + 1) * 128],
                                qk_bf[:, sg, h * 128:(h + 1) * 128], ident[:])
                    qtr2[i_fin] = PE.inc(ins)

            if 0 <= i_nrm < NT_OWN:
                @block.tensor
                def _(tensor, it=it, i_nrm=i_nrm):
                    tensor.wait_ge(s_dve, a2[it])
                    if i_nrm >= 1:
                        tensor.wait_ge(s_act, xv[it - 1])
                    for c in range(16):
                        ins = tensor.transpose(ps_tr[1][:, c * 128:(c + 1) * 128],
                                               xhat[:, c * 128:(c + 1) * 128], ident[:])
                    trpe2[i_nrm] = PE.inc(ins)

            if 0 <= i_fin < NT_OWN or 0 <= i_nrm < NT_OWN:
                @block.scalar
                def _(scalar, it=it, i_fin=i_fin, i_nrm=i_nrm, g=g):
                    if 0 <= i_fin < NT_OWN:
                        scalar.wait_ge(s_pe, qtr2[i_fin])
                        ins = None
                        for sg in range(2):
                            h0 = g * 8 + sg * 4
                            qdst = AP(tensor=Qfm[:].tensor,
                                      offset=Qfm[:].offset + h0 * TOWN + i_fin * 128,
                                      ap=[Qfm[:].ap[0], [TOWN, 4], [1, 128]])
                            ins = scalar.activation(
                                qdst, ps_tr[0][:, sg * 512:(sg + 1) * 512].rearrange("p (h d) -> p h d", h=4, d=128),
                                AF.Copy)
                        b2[it] = ACT.inc(ins)
                    else:
                        b2[it] = ACT.n
                    if 0 <= i_nrm < NT_OWN:
                        scalar.wait_ge(s_pe, trpe2[i_nrm])
                        if i_nrm >= 1:
                            scalar.wait_ge(s_pe, qmm2[i_nrm - 1])
                        scalar.activation(xT[:], ps_tr[1][:], AF.Copy)
                        ins = scalar.activation(xT[:, 0:128], ps_tr[1][:, 0:128], AF.Copy)
                        xv[it] = ACT.inc(ins)
                    else:
                        xv[it] = ACT.n
            else:
                b2[it] = ACT.n
                xv[it] = ACT.n

            @block.vector
            def _(vector, it=it, i_rp=i_rp, i_nrm=i_nrm):
                ins = None
                rp0 = rp1 = None
                if 0 <= i_rp < NT_OWN:
                    vector.wait_ge(s_pe, qmm2[i_rp])
                    vector.wait_ge(d_tb, tb_dma2[i_rp])
                    slot = i_rp % 2
                    rp0 = list(rope_ops(ps_q[slot][:, 0:512], 0, i_rp % 3))
                    rp1 = list(rope_ops(ps_q[slot][:, 512:1024], 1, i_rp % 3))
                    for k in range(3):
                        ins = rp0[k](vector)
                        ins = rp1[k](vector)
                    vector.drain()
                    ins = rp0[3](vector)
                    ins = rp1[3](vector)
                    ins = rp0[4](vector)
                    ins = rp1[4](vector)
                    ins = rp0[5](vector)
                    ins = rp1[5](vector)
                c2[it] = DVE.inc(ins) if ins is not None else DVE.n

            if 0 <= i_nrm < NT_OWN:
                @block.tensor
                def _(tensor, it=it, i_nrm=i_nrm, g=g):
                    tensor.wait_ge(s_act, xv[it])
                    if it >= 1:
                        tensor.wait_ge(s_dve, c2[it - 1])
                    slot = i_nrm % 2
                    for c in range(16):
                        if i_nrm == 0:
                            tensor.wait_ge(s_act, wq_vals[2 * c + 1])
                        lhsT = xT[:, c * 128:(c + 1) * 128]
                        tensor.matmul(ps_q[slot][:, 0:512], lhsT, wqh[:, c, 0:512],
                                      start=(c == 0), stop=(c == 15))
                        ins = tensor.matmul(ps_q[slot][:, 512:1024], lhsT, wqh[:, c, 512:1024],
                                            start=(c == 0), stop=(c == 15))
                    qmm2[i_nrm] = PE.inc(ins)

            if 0 <= i_rp < NT_OWN:
                @block.scalar
                def _(scalar, it=it, i_rp=i_rp):
                    scalar.wait_ge(s_dve, c2[it])
                    scalar.activation(rp_f[:, 0, :], rp_ss[:, 0, :], AF.Ln)
                    scalar.activation(rp_f[:, 1, :], rp_ss[:, 1, :], AF.Ln)
                    scalar.drain()
                    scalar.activation(rp_f[:, 0, :], rp_f[:, 0, :], AF.Exp, scale=0.5)
                    ins = scalar.activation(rp_f[:, 1, :], rp_f[:, 1, :], AF.Exp, scale=0.5)
                    kn2[i_rp] = ACT.inc(ins)

    fence()
    st_p2.close()
    st_p12.close()

    if debug:
        @block.sync
        def _(sync):
            sync.dma_start(out=dbg["Q"], in_=Qfm[:]).then_inc(d_c, 16)
            dma["c"] += 16
            sync.wait_ge(d_c, dma["c"])
    if upto == "p2":
        st_kvq.close()
        st_top.close()
        return nc

    e3 = st_p3.enter_context
    maskL = e3(nc.sbuf_tensor("maskL", [128, 512], BF16))
    maskR = e3(nc.sbuf_tensor("maskR", [128, 512], BF16))
    vb_sb = e3(nc.sbuf_tensor("vb_sb", [128, NT_OWN * NCH], F32))
    pt_ring = e3(nc.sbuf_tensor("pt_ring", [128, 12, 1024], BF16))
    otm = e3(nc.sbuf_tensor("otm", [128, 512], BF16))
    orc = e3(nc.sbuf_tensor("orc", [128, 4], F32))
    ofm = e3(nc.sbuf_tensor("ofm", [128, 2, H * 128], BF16))
    ps_st = [e3(nc.psum_tensor(f"ps_st{k}", [128, 1024], F32)) for k in range(2)]
    ps_pv = [e3(nc.psum_tensor(f"ps_pv{k}", [128, 258], F32)) for k in range(2)]
    ps_ot = e3(nc.psum_tensor("ps_ot", [128, 512], BF16))

    @block.sync
    def _(sync):
        for dst, src in [(maskL, maskL_h), (maskR, maskR_h), (vb_sb, vb_in)]:
            sync.dma_start(out=dst[:], in_=src).then_inc(d_c, 16)
            dma["c"] += 16
    masks_dma = dma["c"]

    def _emit_norm_tail(itpair):
        qt, kv = itpair

        @block.vector
        def _(vector, qt=qt, kv=kv):
            vector.wait_ge(s_pe, PE.n)
            vector.wait_ge(s_pe, st3["otm_pe"])
            for jj in range(2):
                t = ps_pv[jj][:]
                dsrc = AP(tensor=t.tensor, offset=t.offset + 128,
                          ap=[t.ap[0], [129, 2], [1, 1]])
                vector.tensor_copy(orc[:, jj * 2:(jj + 1) * 2], dsrc)
            vector.drain()
            vector.reciprocal(orc[:], orc[:])
            vector.drain()
            for jj in range(2):
                t = ps_pv[jj][:]
                osrc = AP(tensor=t.tensor, offset=t.offset,
                          ap=[t.ap[0], [129, 2], [1, 128]])
                rb = AP(tensor=orc[:].tensor, offset=orc[:].offset + jj * 2,
                        ap=[orc[:].ap[0], [1, 2], [0, 128]])
                ins = vector.tensor_tensor(
                    otm[:, jj * 256:(jj + 1) * 256].rearrange("p (h d) -> p h d", h=2, d=128),
                    osrc, rb, ALU.mult)
            DVE.inc(ins)
        st3["pv_norm"] = DVE.n

        @block.tensor
        def _(tensor):
            tensor.wait_ge(s_dve, st3["pv_norm"])
            tensor.wait_ge(s_dve, st3["ot_evac"])
            for hh in range(4):
                ins = tensor.transpose(ps_ot[:, hh * 128:(hh + 1) * 128],
                                       otm[:, hh * 128:(hh + 1) * 128], ident[:])
            PE.inc(ins)
        st3["otm_pe"] = PE.n

        @block.vector
        def _(vector, qt=qt, kv=kv):
            vector.wait_ge(s_pe, st3["otm_pe"])
            if kv == 0:
                waits(vector, g_ofm[qt % 2])
            ins = vector.tensor_copy(ofm[:, qt % 2, kv * 512:(kv + 1) * 512], ps_ot[:])
            DVE.inc(ins)
        st3["ot_evac"] = DVE.n

        if kv == KVH - 1:
            @block.sync
            def _(sync, qt=qt):
                sync.wait_ge(s_dve, st3["ot_evac"])
                sync.dma_start(out=scr_o[qt], in_=ofm[:, qt % 2, :]).then_inc(d_sc, 16)
                dma["sc"] += 16
            ofm_dma_val[qt] = dma["sc"]
            g_ofm[qt % 2] = [(d_sc, dma["sc"])]

    iters = [(qt, kv) for qt in range(NT_OWN) for kv in range(KVH)]
    NPT = 12
    st3 = {"pv_norm": 0, "otm_pe": 0, "ot_evac": 0}
    rec_exp = {}
    rec_stpe = {}
    rec_pvpe = {}
    rec_mask = {}
    ofm_dma_val = [0] * NT_OWN
    g_ofm = [[], []]

    def pcs(p):
        return [2 * p] + ([2 * p + 1] if 2 * p + 1 < NCH else [])

    for it, (qt, kv) in enumerate(iters):
        act_base = ACT.n

        @block.tensor
        def _(tensor, it=it, qt=qt, kv=kv, act_base=act_base):
            if it == 0:
                tensor.wait_ge(d_c, masks_dma)
            pqt, pkv = iters[it - 1] if it >= 1 else (None, None)
            for p in range(NPAIR):
                g = it * NPAIR + p
                sslot = g % 2
                if p >= 2:
                    tensor.wait_ge(s_act, act_base + p - 1)
                elif it >= 1:
                    tensor.wait_ge(s_act, rec_exp[(it - 1, NPAIR - 2 + p)])
                for c in pcs(p):
                    lhsT = K_sb[:, kv, (qt + c) * 128:(qt + c + 1) * 128]
                    rhs = AP(tensor=Qfm[:].tensor,
                             offset=Qfm[:].offset + (4 * kv) * TOWN + qt * 128,
                             ap=[Qfm[:].ap[0], [TOWN, 4], [1, 128]])
                    ins = tensor.matmul(ps_st[sslot][:, (c % 2) * 512:(c % 2 + 1) * 512],
                                        lhsT, rhs, start=True, stop=True)
                rec_stpe[(it, p)] = PE.inc(ins)
                if it >= 1:
                    pg = (it - 1) * NPAIR + p
                    pslot = pg % NPT
                    for c, req in rec_mask[(it - 1, p)]:
                        tensor.wait_ge(s_act, req["act"])
                        if "dve" in req:
                            tensor.wait_ge(s_dve, req["dve"])
                        if c == 0:
                            tensor.wait_ge(s_dve, st3["pv_norm"])
                        col = (c % 2) * 512
                        for hh in range(4):
                            lhsT = pt_ring[:, pslot, col + hh * 128:col + (hh + 1) * 128]
                            rhs = Vp_sb[:, pqt + c, pkv, :]
                            ins = tensor.matmul(
                                ps_pv[hh // 2][:, (hh % 2) * 129:(hh % 2 + 1) * 129],
                                lhsT, rhs, start=(c == 0 and hh % 2 == 0),
                                stop=(c == NCH - 1))
                    rec_pvpe[(it - 1, p)] = PE.inc(ins)

        if it >= 1:
            _emit_norm_tail(iters[it - 1])

        @block.scalar
        def _(scalar, it=it, act_base=act_base):
            for p in range(NPAIR):
                g = it * NPAIR + p
                sslot = g % 2
                pslot = g % NPT
                scalar.wait_ge(s_pe, rec_stpe[(it, p)])
                og, rem = divmod(g - NPT, NPAIR)
                if g >= NPT and (og, rem) in rec_pvpe:
                    scalar.wait_ge(s_pe, rec_pvpe[(og, rem)])
                ncol = len(pcs(p)) * 512
                scalar.activation(ps_st[sslot][:, 0:ncol], ps_st[sslot][:, 0:ncol],
                                  AF.Tanh, scale=1.0 / (CAP * float(np.sqrt(float(D)))))
                ins = scalar.activation(pt_ring[:, pslot, 0:ncol],
                                        ps_st[sslot][:, 0:ncol], AF.Exp, scale=CAP)
                v = ACT.inc(ins)
                assert v == act_base + p + 1
                rec_exp[(it, p)] = v

        @block.vector
        def _(vector, it=it, qt=qt):
            for p in range(NPAIR):
                g = it * NPAIR + p
                pslot = g % NPT
                auth = []
                for c in pcs(p):
                    tri, needs_vb = _mask_kind(qt, c)
                    col = (c % 2) * 512
                    pa = pt_ring[:, pslot, col:col + 512]
                    vbcol = vb_sb[:, qt * NCH + c:qt * NCH + c + 1]
                    if tri is not None or needs_vb:
                        vector.wait_ge(s_act, rec_exp[(it, p)])
                        if tri is not None and needs_vb:
                            m = maskL[:] if tri == "L" else maskR[:]
                            ins = vector.scalar_tensor_tensor(pa, pa, vbcol, m,
                                                              ALU.mult, ALU.mult)
                        elif tri is not None:
                            m = maskL[:] if tri == "L" else maskR[:]
                            ins = vector.tensor_tensor(pa, pa, m, ALU.mult)
                        else:
                            ins = vector.tensor_scalar(pa, pa, vbcol, None, ALU.mult)
                        auth.append((c, {"act": rec_exp[(it, p)], "dve": DVE.inc(ins)}))
                    else:
                        auth.append((c, {"act": rec_exp[(it, p)]}))
                rec_mask[(it, p)] = auth

    it = len(iters)

    @block.tensor
    def _(tensor):
        pqt, pkv = iters[-1]
        for p in range(NPAIR):
            pg = (it - 1) * NPAIR + p
            pslot = pg % NPT
            for c, req in rec_mask[(it - 1, p)]:
                tensor.wait_ge(s_act, req["act"])
                if "dve" in req:
                    tensor.wait_ge(s_dve, req["dve"])
                if c == 0:
                    tensor.wait_ge(s_dve, st3["pv_norm"])
                col = (c % 2) * 512
                for hh in range(4):
                    lhsT = pt_ring[:, pslot, col + hh * 128:col + (hh + 1) * 128]
                    rhs = Vp_sb[:, pqt + c, pkv, :]
                    ins = tensor.matmul(
                        ps_pv[hh // 2][:, (hh % 2) * 129:(hh % 2 + 1) * 129],
                        lhsT, rhs, start=(c == 0 and hh % 2 == 0),
                        stop=(c == NCH - 1))
            rec_pvpe[(it - 1, p)] = PE.inc(ins)

    _emit_norm_tail(iters[-1])

    fence()
    st_p3.close()
    st_kvq.close()

    if debug:
        for qt in range(NT_OWN):
            @block.sync
            def _(sync, qt=qt):
                sync.dma_start(out=dbg["O"][qt], in_=scr_o[qt]).then_inc(d_c, 16)
                dma["c"] += 16
        @block.sync
        def _(sync):
            sync.wait_ge(d_c, dma["c"])
    if upto == "p3":
        st_top.close()
        return nc

    scr_y1 = nc.dram_tensor("scr_y1", [NT_OWN, 128, E], F32, kind="Internal").ap()
    rs2 = en(nc.sbuf_tensor("rs2", [128, NT_OWN], F32))
    m2_ = en(nc.sbuf_tensor("m2_", [128, NT_OWN], F32))
    var2 = en(nc.sbuf_tensor("var2", [128, NT_OWN], F32))
    msq2 = en(nc.sbuf_tensor("msq2", [128, NT_OWN], F32))

    stW = ExitStack()
    eW = stW.enter_context
    wbra_sb = eW(nc.sbuf_tensor("wbra_sb", [128, 16, 2120], BF16))
    bflat_sb = eW(nc.sbuf_tensor("bflat_sb", [64, 2048], BF16))
    bb_stage = eW(nc.sbuf_tensor("bb_stage", [1, E], F32))
    bb_bf = eW(nc.sbuf_tensor("bb_bf", [1, E], BF16))
    onesc = eW(nc.sbuf_tensor("onesc", [1, 128], BF16))
    wstg45 = eW(nc.sbuf_tensor("wstg45", [128, 3, 1024], F32))

    stA = ExitStack()
    eA = stA.enter_context
    wo_sb = eA(nc.sbuf_tensor("wo_sb", [128, 16, 2048], BF16))
    ot_sb = eA(nc.sbuf_tensor("ot_sb", [128, 2, H * 128], BF16))
    y1b = eA(nc.sbuf_tensor("y1b", [128, 2, E], F32))
    stA_sum = eA(nc.sbuf_tensor("stA_sum", [128, NT_OWN], F32))
    stA_ss = eA(nc.sbuf_tensor("stA_ss", [128, NT_OWN], F32))
    ps_wo = [eA(nc.psum_tensor(f"ps_wo{k}", [128, E], F32)) for k in range(2)]

    wo_r = wo_in.rearrange("(c p) f -> c p f", p=128)
    pieces = []
    for c in range(16):
        for n in range(2):
            pieces.append((wo_r[c][:, n * 1024:(n + 1) * 1024], wo_sb[:, c, n * 1024:(n + 1) * 1024]))
    n0 = len(stage_hist)
    stream_weight(pieces, staging=wstg45, nslots=3)
    wo_vals = stage_hist[n0:]

    @block.sync
    def _(sync):
        for dst, src in [(bb_stage, bb_in), (onesc, ones_h)]:
            sync.dma_start(out=dst[:], in_=src).then_inc(d_c, 16)
            dma["c"] += 16
    bbdma = dma["c"]

    @block.scalar
    def _(scalar):
        scalar.wait_ge(d_c, bbdma)
        ins = scalar.activation(bb_bf[:], bb_stage[:], AF.Copy)
        ACT.inc(ins)
    bb_ready = ACT.n

    wbra_r = wbra_in.rearrange("(c p) f -> c p f", p=128)
    pieces = []
    for c in range(16):
        for n in range(2):
            pieces.append((wbra_r[c][:, n * 1024:(n + 1) * 1024], wbra_sb[:, c, n * 1024:(n + 1) * 1024]))
        pieces.append((wbra_r[c][:, 2048:2120], wbra_sb[:, c, 2048:2120]))
    for n in range(2):
        pieces.append((bflat_in[:, n * 1024:(n + 1) * 1024], bflat_sb[:, n * 1024:(n + 1) * 1024]))
    n0 = len(stage_hist)
    stream_weight(pieces, staging=wstg45, nslots=3)
    ffn_ready = stage_hist[-1]

    woA = [0] * NT_OWN
    y1A = [0] * NT_OWN
    y1_dma = [0] * NT_OWN
    o_dma = [0] * NT_OWN
    x_dmaA = [0] * NT_OWN
    rsg_val = [0] * (NT_OWN // 4)

    for i in range(NT_OWN):
        slot = i % 2
        hi = i + 8

        @block.sync
        def _(sync, i=i, slot=slot, hi=hi):
            if i >= 2:
                sync.wait_ge(s_dve, y1A[i - 2])
            sync.wait_ge(d_sc, ofm_dma_val[i])
            sync.dma_start(out=ot_sb[:, slot, :], in_=scr_o[i]).then_inc(d_sc, 16)
            dma["sc"] += 16
            o_dma[i] = dma["sc"]
            sync.dma_start(out=xbuf[:, slot, :], in_=xh[hi * 128:(hi + 1) * 128, :]).then_inc(d_x, 16)
            dma["x"] += 16
            x_dmaA[i] = dma["x"]

        @block.tensor
        def _(tensor, i=i, slot=slot):
            tensor.wait_ge(d_sc, o_dma[i])
            if i >= 2:
                tensor.wait_ge(s_dve, y1A[i - 2])
            for c in range(16):
                if i == 0:
                    tensor.wait_ge(s_act, wo_vals[2 * c + 1])
                lhsT = ot_sb[:, slot, c * 128:(c + 1) * 128]
                for n in range(4):
                    ins = tensor.matmul(ps_wo[slot][:, n * 512:(n + 1) * 512], lhsT,
                                        wo_sb[:, c, n * 512:(n + 1) * 512],
                                        start=(c == 0), stop=(c == 15))
            woA[i] = PE.inc(ins)

        @block.vector
        def _(vector, i=i, slot=slot):
            vector.wait_ge(s_pe, woA[i])
            vector.wait_ge(d_x, x_dmaA[i])
            vector.scalar_tensor_tensor(y1b[:, slot, :], ps_wo[slot][:], 1.0,
                                        xbuf[:, slot, :], ALU.mult, ALU.add,
                                        accum_out=stA_sum[:, i:i + 1])
            vector.drain()
            ins = vector.scalar_tensor_tensor(xhat[:], y1b[:, slot, :], 1.0,
                                              y1b[:, slot, :], ALU.mult, ALU.mult,
                                              accum_out=stA_ss[:, i:i + 1])
            y1A[i] = DVE.inc(ins)

        @block.sync
        def _(sync, i=i, slot=slot):
            sync.wait_ge(s_dve, y1A[i])
            sync.dma_start(out=scr_y1[i], in_=y1b[:, slot, :]).then_inc(d_y1, 16)
            dma["y1"] += 16
            y1_dma[i] = dma["y1"]

        if i % 4 == 3:
            g = i // 4

            @block.vector
            def _(vector, g=g, i=i):
                vector.wait_ge(s_dve, y1A[i])
                vector.drain()
                vector.tensor_scalar(m2_[:, g * 4:g * 4 + 4], stA_sum[:, g * 4:g * 4 + 4],
                                     1.0 / E, None, ALU.mult)
                vector.drain()
                vector.tensor_tensor(msq2[:, g * 4:g * 4 + 4], m2_[:, g * 4:g * 4 + 4],
                                     m2_[:, g * 4:g * 4 + 4], ALU.mult)
                vector.drain()
                ins = vector.scalar_tensor_tensor(var2[:, g * 4:g * 4 + 4],
                                                  stA_ss[:, g * 4:g * 4 + 4], 1.0 / E,
                                                  msq2[:, g * 4:g * 4 + 4],
                                                  ALU.mult, ALU.subtract)
                DVE.inc(ins)

            @block.scalar
            def _(scalar, g=g):
                scalar.wait_ge(s_dve, DVE.n)
                scalar.activation(var2[:, g * 4:g * 4 + 4], var2[:, g * 4:g * 4 + 4],
                                  AF.Ln, bias=eps_ln[:])
                scalar.drain()
                ins = scalar.activation(rs2[:, g * 4:g * 4 + 4], var2[:, g * 4:g * 4 + 4],
                                        AF.Exp, scale=-0.5)
                rsg_val[g] = ACT.inc(ins)

    a_done_pe = woA[NT_OWN - 1]
    a_done_dve = y1A[NT_OWN - 1]
    a_done_dma = y1_dma[NT_OWN - 1]
    stA.close()

    stB = ExitStack()
    eB = stB.enter_context
    y1c = eB(nc.sbuf_tensor("y1c", [128, 3, E], F32))
    xT2 = eB(nc.sbuf_tensor("xT2", [128, 2, E], BF16))
    hpr2 = eB(nc.sbuf_tensor("hpr2", [128, 2, 64], BF16))
    lgha = eB(nc.sbuf_tensor("lgha", [128, 72], F32))
    lg = eB(nc.sbuf_tensor("lg", [128, 8], F32))
    gm1 = eB(nc.sbuf_tensor("gm1", [128, 1], F32))
    gnm1 = eB(nc.sbuf_tensor("gnm1", [128, 1], F32))
    gm2 = eB(nc.sbuf_tensor("gm2", [128, 1], F32))
    gtmp = eB(nc.sbuf_tensor("gtmp", [128, 8], F32))
    gwex = eB(nc.sbuf_tensor("gwex", [128, 8], F32))
    gsum = eB(nc.sbuf_tensor("gsum", [128, 1], F32))
    hT = eB(nc.sbuf_tensor("hT", [64, 128], BF16))
    ps_ffn = eB(nc.psum_tensor("ps_ffn", [128, E], F32))
    ps_trB = eB(nc.psum_tensor("ps_trB", [128, E], BF16))
    ps_ra = eB(nc.psum_tensor("ps_ra", [128, 72], F32))
    ps_hT = eB(nc.psum_tensor("ps_hT", [64, 128], BF16))

    ffn_done = [0] * NT_OWN
    out_dma = [0] * NT_OWN
    raB = [0] * NT_OWN
    ra_ev = [0] * NT_OWN
    normB = [0] * NT_OWN
    trB = [0] * NT_OWN
    xTB = [0] * NT_OWN
    mmB = [0] * NT_OWN
    gateB = [0] * NT_OWN
    hTpe = [0] * NT_OWN
    hTev = [0] * NT_OWN
    y1_dma2 = [0] * NT_OWN

    def emit_lagged(j):
        slot = j % 2
        slot3 = j % 3

        @block.tensor
        def _(tensor, j=j, slot=slot):
            tensor.wait_ge(s_dve, gateB[j])
            if j >= 1:
                tensor.wait_ge(s_dve, hTev[j - 1])
            ins = tensor.transpose(ps_hT[:], hpr2[:, slot, :], ident[:])
            hTpe[j] = PE.inc(ins)

        @block.vector
        def _(vector, j=j):
            vector.wait_ge(s_pe, hTpe[j])
            ins = vector.tensor_copy(hT[:], ps_hT[:])
            hTev[j] = DVE.inc(ins)

        @block.tensor
        def _(tensor, j=j, slot=slot):
            if j == 0:
                tensor.wait_ge(s_act, bb_ready)
            if j >= 1:
                tensor.wait_ge(s_dve, ffn_done[j - 1])
            for n in range(4):
                ins = tensor.matmul(ps_ffn[:, n * 512:(n + 1) * 512], onesc[:],
                                    bb_bf[:, n * 512:(n + 1) * 512],
                                    start=True, stop=False)
            for c in range(16):
                lhsT = xT2[:, slot, c * 128:(c + 1) * 128]
                for n in range(4):
                    ins = tensor.matmul(ps_ffn[:, n * 512:(n + 1) * 512], lhsT,
                                        wbra_sb[:, c, n * 512:(n + 1) * 512],
                                        start=False, stop=False)
            tensor.wait_ge(s_dve, hTev[j])
            for n in range(4):
                ins = tensor.matmul(ps_ffn[:, n * 512:(n + 1) * 512], hT[:],
                                    bflat_sb[:, n * 512:(n + 1) * 512],
                                    start=False, stop=True)
            mmB[j] = PE.inc(ins)

        @block.vector
        def _(vector, j=j, slot3=slot3):
            vector.wait_ge(s_pe, mmB[j])
            ins = vector.tensor_add(y1c[:, slot3, :], y1c[:, slot3, :], ps_ffn[:])
            ffn_done[j] = DVE.inc(ins)

        @block.gpsimd
        def _(gpsimd, j=j, slot3=slot3):
            gpsimd.wait_ge(s_dve, ffn_done[j])
            gpsimd.dma_start(out=out[j * 128:(j + 1) * 128, :], in_=y1c[:, slot3, :]).then_inc(d_out, 16)
            dma["out"] += 16
            out_dma[j] = dma["out"]

    for i in range(NT_OWN):
        slot = i % 2
        slot3 = i % 3

        @block.sync
        def _(sync, i=i, slot3=slot3):
            if i < 2:
                sync.wait_ge(s_pe, a_done_pe)
                sync.wait_ge(s_dve, a_done_dve)
                sync.wait_ge(d_y1, a_done_dma)
                sync.wait_ge(s_act, rsg_val[NT_OWN // 4 - 1])
            if i >= 3:
                sync.wait_ge(s_dve, ffn_done[i - 3])
                sync.wait_ge(d_out, out_dma[i - 3])
            sync.wait_ge(d_y1, y1_dma[i])
            sync.dma_start(out=y1c[:, slot3, :], in_=scr_y1[i]).then_inc(d_sc, 16)
            dma["sc"] += 16
            y1_dma2[i] = dma["sc"]

        @block.vector
        def _(vector, i=i, slot3=slot3):
            vector.wait_ge(d_sc, y1_dma2[i])
            vector.wait_ge(s_act, rsg_val[i // 4])
            if i >= 1:
                vector.wait_ge(s_pe, trB[i - 1])
            rsb = AP(tensor=rs2[:].tensor, offset=rs2[:].offset + i,
                     ap=[rs2[:].ap[0], [0, E]])
            ins = vector.scalar_tensor_tensor(xhat[:], y1c[:, slot3, :],
                                              m2_[:, i:i + 1], rsb,
                                              ALU.subtract, ALU.mult)
            normB[i] = DVE.inc(ins)

        @block.tensor
        def _(tensor, i=i):
            tensor.wait_ge(s_dve, normB[i])
            if i >= 1:
                tensor.wait_ge(s_dve, xTB[i - 1])
            for c in range(16):
                ins = tensor.transpose(ps_trB[:, c * 128:(c + 1) * 128],
                                       xhat[:, c * 128:(c + 1) * 128], ident[:])
            trB[i] = PE.inc(ins)

        @block.vector
        def _(vector, i=i, slot=slot):
            vector.wait_ge(s_pe, trB[i])
            if i >= 2:
                vector.wait_ge(s_pe, mmB[i - 2])
            ins = vector.tensor_copy(xT2[:, slot, :], ps_trB[:])
            xTB[i] = DVE.inc(ins)

        @block.tensor
        def _(tensor, i=i, slot=slot):
            tensor.wait_ge(s_dve, xTB[i])
            if i == 0:
                tensor.wait_ge(s_act, ffn_ready)
            if i >= 1:
                tensor.wait_ge(s_dve, ra_ev[i - 1])
            for c in range(16):
                ins = tensor.matmul(ps_ra[:], xT2[:, slot, c * 128:(c + 1) * 128],
                                    wbra_sb[:, c, 2048:2120],
                                    start=(c == 0), stop=(c == 15))
            raB[i] = PE.inc(ins)

        @block.vector
        def _(vector, i=i):
            vector.wait_ge(s_pe, raB[i])
            ins = vector.tensor_copy(lgha[:], ps_ra[:])
            ra_ev[i] = DVE.inc(ins)
            vector.drain()
            vector.tensor_copy(lg[:], lgha[:, 0:8])
            vector.drain()
            vector.tensor_reduce(gm1[:], lg[:], AX.X, ALU.max)
            vector.drain()
            vector.tensor_scalar(gtmp[:], lg[:], gm1[:], -1e30, ALU.is_ge, ALU.mult)
            vector.drain()
            vector.tensor_add(gtmp[:], gtmp[:], lg[:])
            vector.drain()
            vector.tensor_reduce(gm2[:], gtmp[:], AX.X, ALU.max)
            vector.drain()
            ins = vector.tensor_scalar(gnm1[:], gm1[:], -1.0, None, ALU.mult)
            DVE.inc(ins)

        @block.scalar
        def _(scalar):
            scalar.wait_ge(s_dve, DVE.n)
            ins = scalar.activation(gwex[:], lg[:], AF.Exp, bias=gnm1[:])
            ACT.inc(ins)

        @block.vector
        def _(vector, i=i, slot=slot):
            vector.wait_ge(s_act, ACT.n)
            vector.tensor_scalar(gtmp[:], lg[:], gm2[:], None, ALU.is_ge)
            vector.drain()
            vector.tensor_mul(gwex[:], gwex[:], gtmp[:])
            vector.drain()
            vector.tensor_reduce(gsum[:], gwex[:], AX.X, ALU.add)
            vector.drain()
            vector.reciprocal(gsum[:], gsum[:])
            vector.drain()
            vector.tensor_scalar(gsum[:], gsum[:], SCALE_MOE, None, ALU.mult)
            vector.drain()
            vector.tensor_scalar(gwex[:], gwex[:], gsum[:], None, ALU.mult)
            vector.drain()
            if i >= 2:
                vector.wait_ge(s_pe, hTpe[i - 2])
            wb = AP(tensor=gwex[:].tensor, offset=gwex[:].offset,
                    ap=[gwex[:].ap[0], [1, 8], [0, 8]])
            ins = vector.tensor_tensor(hpr2[:, slot, :].rearrange("p (e r) -> p e r", e=8),
                                       lgha[:, 8:72].rearrange("p (e r) -> p e r", e=8),
                                       wb, ALU.mult)
            gateB[i] = DVE.inc(ins)

        if i >= 1:
            emit_lagged(i - 1)
    emit_lagged(NT_OWN - 1)

    @block.sync
    def _(sync):
        sync.wait_ge(d_out, dma["out"])

    stB.close()
    stW.close()
    st_p45.close()
    st_top.close()
    return nc


_CACHE = {}


def make_in_maps(inputs):
    x = _f32(inputs["L0_attn_hidden_states"])
    wq = _f32(inputs["L0_attn_wq"])
    wk = _f32(inputs["L0_attn_wk"])
    wv = _f32(inputs["L0_attn_wv"])
    wo = _f32(inputs["L0_attn_wo"])
    qscale = _f32(inputs["L0_attn_query_scale"])
    kscale = _f32(inputs["L0_attn_key_scale"])
    W_base = _f32(inputs["L0_ffn_W_base"])
    b_base = _f32(inputs["L0_ffn_b_base"])
    W_router = _f32(inputs["L0_ffn_W_router"])
    A_w = _f32(inputs["L0_ffn_A_w"])
    B_w = _f32(inputs["L0_ffn_B_w"])

    wbra = _f32(np.concatenate(
        [W_base.T, W_router.T, A_w.reshape(NEXP * RANK, E).T], axis=1))
    bflat = _f32(np.transpose(B_w, (0, 2, 1)).reshape(NEXP * RANK, E))
    b_bcast = _f32(b_base[None, :])
    qs_rep = _f32(np.tile(qscale, (128, H)))
    ks_rep = _f32(np.tile(kscale, (128, KVH)))
    inv_freq = 1.0 / (10000.0 ** (np.arange(0, D // 2, dtype=np.float64) / (D // 2)))

    in_maps = []
    for core in range(NCORE):
        b = core // 4
        q4 = core % 4
        own0 = q4 * TOWN
        lo, hi = own0 - WIN, own0 + TOWN + WIN
        lo_c, hi_c = max(lo, 0), min(hi, S)
        xs = x[b, lo_c:hi_c, :]
        pad_l, pad_r = lo_c - lo, hi - hi_c
        if pad_l or pad_r:
            xs = np.concatenate(
                [np.repeat(xs[:1], pad_l, 0), xs, np.repeat(xs[-1:], pad_r, 0)], 0)
        pos = np.clip(np.arange(lo, hi, dtype=np.float64), 0, S - 1)
        ang = np.outer(pos, inv_freq)
        cos_t = _f32(np.concatenate([np.cos(ang), np.cos(ang)], 1))
        sin_t = _f32(np.concatenate([-np.sin(ang), np.sin(ang)], 1))
        vb = np.zeros((NT_OWN, NCH), dtype=np.float64)
        for qt in range(NT_OWN):
            for c in range(NCH):
                kb = own0 - WIN + 128 * (qt + c)
                vb[qt, c] = 1.0 if (0 <= kb <= S - 128) else 0.0
        vb = _f32(np.broadcast_to(vb.reshape(1, -1), (128, NT_OWN * NCH)))
        in_maps.append({
            "xh": _f32(xs), "cos_t": cos_t, "sin_t": sin_t, "vb": vb,
            "wq": wq, "wk": wk, "wv": wv, "wo": wo,
            "wbra": wbra, "bflat": bflat, "b_bcast": b_bcast,
            "qscale_rep": qs_rep, "kscale_rep": ks_rep,
        })
    return in_maps


def kernel(**inputs):
    in_maps = make_in_maps(inputs)
    if "nc" not in _CACHE:
        _CACHE["nc"] = build_program()
    nc = _CACHE["nc"]
    res = run_bass_kernel_spmd(nc, in_maps, core_ids=list(range(NCORE)))
    outp = np.empty((B, S, E), dtype=np.float32)
    for core in range(NCORE):
        b = core // 4
        q4 = core % 4
        outp[b, q4 * TOWN:(q4 + 1) * TOWN, :] = res.results[core]["out"]
    return outp


# version 57 (final)
# speedup vs baseline: 1.0321x; 1.0102x over previous
import numpy as np
import ml_dtypes
from contextlib import ExitStack

import concourse.bass as bass
import concourse.mybir as mybir
from concourse.bass_utils import run_bass_kernel_spmd

F32 = mybir.dt.float32
BF16 = mybir.dt.bfloat16
AF = mybir.ActivationFunctionType
ALU = mybir.AluOpType
AX = mybir.AxisListType
AP = bass.AP

B, S, E = 2, 8192, 2048
H, KVH, D = 16, 4, 128
WIN = 1024
CAP = 50.0
EPS_ATT = 1e-6
EPS_LN = 1e-5
NCORE = 8
TOWN = 2048
HALO = 4096
NT_OWN = TOWN // 128
NT_HALO = HALO // 128
NCH = 17
NPAIR = 9
NEXP, RANK = 8, 8
SCALE_MOE = 2.0


def _f32(x):
    return np.ascontiguousarray(x, dtype=np.float32)


def _mask_kind(qt, c):
    tri = "L" if c == 0 else ("R" if c == 16 else None)
    vb = (qt + c <= 7) or (qt + c >= 24)
    return tri, vb


class Ctr:
    def __init__(self, sem):
        self.sem = sem
        self.n = 0

    def inc(self, ins):
        ins.then_inc(self.sem, 1)
        self.n += 1
        return self.n


def build_program(upto="full", debug=False):
    nc = bass.Bass("TRN2", target_bir_lowering=False)

    xh = nc.dram_tensor("xh", [HALO, E], F32, kind="ExternalInput").ap()
    cos_in = nc.dram_tensor("cos_t", [HALO, D], F32, kind="ExternalInput").ap()
    sin_in = nc.dram_tensor("sin_t", [HALO, D], F32, kind="ExternalInput").ap()
    vb_in = nc.dram_tensor("vb", [128, NT_OWN * NCH], F32, kind="ExternalInput").ap()
    wq_in = nc.dram_tensor("wq", [E, H * D], F32, kind="ExternalInput").ap()
    wk_in = nc.dram_tensor("wk", [E, KVH * D], F32, kind="ExternalInput").ap()
    wv_in = nc.dram_tensor("wv", [E, KVH * D], F32, kind="ExternalInput").ap()
    wo_in = nc.dram_tensor("wo", [H * D, E], F32, kind="ExternalInput").ap()
    wbra_in = nc.dram_tensor("wbra", [E, 2120], F32, kind="ExternalInput").ap()
    bflat_in = nc.dram_tensor("bflat", [NEXP * RANK, E], F32, kind="ExternalInput").ap()
    bb_in = nc.dram_tensor("b_bcast", [1, E], F32, kind="ExternalInput").ap()
    qs_in = nc.dram_tensor("qscale_rep", [128, H * D], F32, kind="ExternalInput").ap()
    ks_in = nc.dram_tensor("kscale_rep", [128, KVH * D], F32, kind="ExternalInput").ap()

    scr_o = nc.dram_tensor("scr_o", [NT_OWN, 128, H * 128], BF16, kind="Internal").ap()
    out = nc.dram_tensor("out", [TOWN, E], F32, kind="ExternalOutput").ap()

    ident_np = np.eye(128, dtype=np.float32).astype(ml_dtypes.bfloat16)
    jj_ = np.arange(128)[:, None]
    ii_ = np.arange(128)[None, :]
    maskL_np = np.tile((jj_ >= ii_).astype(np.float32), (1, 4)).astype(ml_dtypes.bfloat16)
    maskR_np = np.tile((jj_ <= ii_).astype(np.float32), (1, 4)).astype(ml_dtypes.bfloat16)
    ident_h = nc.inline_tensor(ident_np, name="c_ident").ap()
    ones_h = nc.inline_tensor(np.ones((1, 128), dtype=np.float32).astype(ml_dtypes.bfloat16),
                              name="c_ones").ap()
    maskL_h = nc.inline_tensor(maskL_np, name="c_maskL").ap()
    maskR_h = nc.inline_tensor(maskR_np, name="c_maskR").ap()

    dbg = {}
    if debug:
        dbg["K"] = nc.dram_tensor("dbg_K", [128, KVH, HALO], BF16, kind="ExternalOutput").ap()
        dbg["V"] = nc.dram_tensor("dbg_V", [128, NT_HALO, KVH, 129], BF16, kind="ExternalOutput").ap()
        dbg["Q"] = nc.dram_tensor("dbg_Q", [128, H, TOWN], BF16, kind="ExternalOutput").ap()
        dbg["O"] = nc.dram_tensor("dbg_O", [NT_OWN, 128, H * 128], BF16, kind="ExternalOutput").ap()
        dbg["XH"] = nc.dram_tensor("dbg_XH", [128, E], BF16, kind="ExternalOutput").ap()
        dbg["QR"] = nc.dram_tensor("dbg_QR", [128, 1024], F32, kind="ExternalOutput").ap()

    st_top = ExitStack()
    st_kvq = ExitStack()
    st_p12 = ExitStack()
    st_p1 = ExitStack()
    st_p2 = ExitStack()
    st_p3 = ExitStack()
    st_p45 = ExitStack()

    en = st_top.enter_context
    ident = en(nc.sbuf_tensor("ident", [128, 128], BF16))
    rs_all = en(nc.sbuf_tensor("rs_all", [128, NT_HALO], F32))
    m_all = en(nc.sbuf_tensor("m_all", [128, NT_HALO], F32))
    eps_ln = en(nc.sbuf_tensor("eps_ln", [128, 1], F32))
    xbuf = en(nc.sbuf_tensor("xbuf", [128, 2, E], F32))
    xhat = en(nc.sbuf_tensor("xhat", [128, E], BF16))

    ek = st_kvq.enter_context
    K_sb = ek(nc.sbuf_tensor("K_sb", [128, KVH, HALO], BF16))
    Vp_sb = ek(nc.sbuf_tensor("Vp_sb", [128, NT_HALO, KVH, 129], BF16))
    Qfm = ek(nc.sbuf_tensor("Qfm", [128, H, TOWN], BF16))

    e12 = st_p12.enter_context
    wstg = e12(nc.sbuf_tensor("wstg", [128, 3, 512], F32))
    xT = e12(nc.sbuf_tensor("xT", [128, E], BF16))
    tbl = e12(nc.sbuf_tensor("tbl", [128, 3, 2, D], F32))
    rp_rot = e12(nc.sbuf_tensor("rp_rot", [128, 2, 512], F32))
    rp_t1 = e12(nc.sbuf_tensor("rp_t1", [128, 2, 512], F32))
    rp_ss = e12(nc.sbuf_tensor("rp_ss", [128, 2, 4], F32))
    rp_f = e12(nc.sbuf_tensor("rp_f", [128, 2, 4], F32))
    qk_bf = e12(nc.sbuf_tensor("qk_bf", [128, 2, 512], BF16))
    ps_tr = [e12(nc.psum_tensor(f"ps_tr{k}", [128, E], BF16)) for k in range(2)]

    d_x = en(nc.semaphore("d_x"))
    d_tb = en(nc.semaphore("d_tb"))
    d_w = en(nc.semaphore("d_w"))
    d_w2 = en(nc.semaphore("d_w2"))
    d_y1 = en(nc.semaphore("d_y1"))
    d_sc = en(nc.semaphore("d_sc"))
    d_out = en(nc.semaphore("d_out"))
    d_c = en(nc.semaphore("d_c"))
    s_pe = en(nc.semaphore("s_pe"))
    s_act = en(nc.semaphore("s_act"))
    s_dve = en(nc.semaphore("s_dve"))
    s_pool = en(nc.semaphore("s_pool"))

    block = en(nc.Block())

    PE = Ctr(s_pe)
    ACT = Ctr(s_act)
    DVE = Ctr(s_dve)
    POOL = Ctr(s_pool)
    dma = {"x": 0, "tb": 0, "w": 0, "w2": 0, "sc": 0, "out": 0, "c": 0, "y1": 0}

    def fence():
        marks = [(s_pe, PE.n), (s_act, ACT.n), (s_dve, DVE.n), (s_pool, POOL.n)]
        dmas = [(d_x, dma["x"]), (d_tb, dma["tb"]), (d_w, dma["w"]),
                (d_w2, dma["w2"]), (d_sc, dma["sc"]), (d_out, dma["out"]),
                (d_c, dma["c"]), (d_y1, dma["y1"])]
        for name in ("tensor", "scalar", "vector", "gpsimd", "sync"):
            @getattr(block, name)
            def _(eng):
                for sem, v in marks + dmas:
                    if v:
                        eng.wait_ge(sem, v)

    def waits(eng, guards):
        for sem, v in guards:
            if v:
                eng.wait_ge(sem, v)

    stage_hist = []

    def stream_weight(pieces, staging=None, nslots=3):
        stg = staging if staging is not None else wstg
        for src, dst in pieces:
            slot = len(stage_hist) % nslots
            qidx = len(stage_hist) % 2
            qname = ("gpsimd", "scalar")[qidx]
            sem, key = ((d_w, "w"), (d_w2, "w2"))[qidx]
            npart = src.shape[0]
            ncol = src.shape[-1]

            @getattr(block, qname)
            def _(eng, slot=slot, src=src, ncol=ncol, npart=npart, sem=sem, key=key):
                if len(stage_hist) >= nslots - 1:
                    eng.wait_ge(s_act, stage_hist[-(nslots - 1)])
                eng.dma_start(out=stg[0:npart, slot, 0:ncol], in_=src).then_inc(sem, 16)
                dma[key] += 16
            wval = dma[key]

            @block.scalar
            def _(scalar, slot=slot, dst=dst, ncol=ncol, npart=npart, sem=sem, wval=wval):
                scalar.wait_ge(sem, wval)
                ins = scalar.activation(dst, stg[0:npart, slot, 0:ncol], AF.Copy)
                stage_hist.append(ACT.inc(ins))
        return stage_hist[-1]

    e1 = st_p1.enter_context
    wkv = e1(nc.sbuf_tensor("wkv", [128, 16, 1024], BF16))
    ks_sb = e1(nc.sbuf_tensor("ks_sb", [128, KVH * D], F32))
    ps_kv = [e1(nc.psum_tensor(f"ps_kv{k}", [128, 1024], F32)) for k in range(2)]

    @block.sync
    def _(sync):
        for dst, src in [(ident, ident_h), (ks_sb, ks_in)]:
            sync.dma_start(out=dst[:], in_=src).then_inc(d_c, 16)
            dma["c"] += 16
    const_dma = dma["c"]

    wk_r = wk_in.rearrange("(c p) f -> c p f", p=128)
    wv_r = wv_in.rearrange("(c p) f -> c p f", p=128)
    pieces = []
    for c in range(16):
        pieces.append((wk_r[c], wkv[:, c, 0:512]))
        pieces.append((wv_r[c], wkv[:, c, 512:1024]))
    wkv_ready = stream_weight(pieces)

    @block.vector
    def _(vector):
        ins = vector.memset(Vp_sb[:, :, :, 128:129], 1.0)
        DVE.inc(ins)
    vp_ones = DVE.n

    def tbl_aps(tslot):
        cosb = AP(tensor=tbl[:].tensor, offset=tbl[:].offset + tslot * 2 * D,
                  ap=[tbl[:].ap[0], [0, 4], [1, D]])
        sinb = AP(tensor=tbl[:].tensor, offset=tbl[:].offset + tslot * 2 * D + D,
                  ap=[tbl[:].ap[0], [0, 4], [1, D]])
        return cosb, sinb

    def v4(ap2d):
        return ap2d.rearrange("p (h d) -> p h d", h=4, d=D)

    def rope_ops(kp, rslot, tslot):
        kps = kp.rearrange("p (h s d) -> p h s d", h=4, s=2, d=64)
        rot_in = AP(tensor=kps.tensor, offset=kps.offset + 64,
                    ap=[kps.ap[0], kps.ap[1], [-64, 2], [1, 64]])
        rot = rp_rot[:, rslot, :]
        t1 = rp_t1[:, rslot, :]
        cosb, sinb = tbl_aps(tslot)
        yield lambda v: v.tensor_copy(rot.rearrange("p (h s d) -> p h s d", h=4, s=2, d=64), rot_in)
        yield lambda v: v.tensor_tensor(v4(t1), v4(kp), cosb, ALU.mult)
        yield lambda v: v.tensor_tensor(v4(rot), v4(rot), sinb, ALU.mult)
        yield lambda v: v.tensor_add(t1, t1, rot)
        yield lambda v: v.tensor_tensor(rot, t1, t1, ALU.mult)
        yield lambda v: v.tensor_reduce(rp_ss[:, rslot, :], v4(rot), AX.X, ALU.add)

    def apply_ops(rslot, scale_ap, qk_dst):
        f = rp_f[:, rslot, :]
        t1 = rp_t1[:, rslot, :]
        rot = rp_rot[:, rslot, :]
        fb = AP(tensor=f.tensor, offset=f.offset, ap=[f.ap[0], [1, 4], [0, D]])
        yield lambda v: v.tensor_scalar(f, f, EPS_ATT, None, ALU.add)
        yield lambda v: v.reciprocal(f, f)
        yield lambda v: v.tensor_tensor(v4(rot), v4(t1), fb, ALU.mult)
        yield lambda v: v.tensor_tensor(qk_dst, rot, scale_ap, ALU.mult)

    def emit_rope1(vector, kp, rslot, tslot):
        ops = list(rope_ops(kp, rslot, tslot))
        ops[0](vector)
        ops[1](vector)
        vector.drain()
        ops[2](vector)
        vector.drain()
        ops[3](vector)
        vector.drain()
        ops[4](vector)
        vector.drain()
        return ops[5](vector)

    def emit_apply(vector, rslot, scale_ap):
        ops = list(apply_ops(rslot, scale_ap, qk_bf[:, rslot, :]))
        ops[0](vector)
        vector.drain()
        ops[1](vector)
        vector.drain()
        ops[2](vector)
        vector.drain()
        return ops[3](vector)

    @block.vector
    def _(vector):
        ins = vector.memset(eps_ln[:], EPS_LN)
        DVE.inc(ins)

    pp_sum = e1(nc.sbuf_tensor("pp_sum", [128, NT_HALO], F32))
    pp_ss = e1(nc.sbuf_tensor("pp_ss", [128, NT_HALO], F32))
    pp_dma = [0] * NT_HALO
    pp_val = [0] * NT_HALO
    for i in range(NT_HALO):
        slot = i % 2

        @block.sync
        def _(sync, i=i, slot=slot):
            if i >= 2:
                sync.wait_ge(s_dve, pp_val[i - 2])
            sync.dma_start(out=xbuf[:, slot, :], in_=xh[i * 128:(i + 1) * 128, :]).then_inc(d_x, 16)
            dma["x"] += 16
            pp_dma[i] = dma["x"]

        @block.vector
        def _(vector, i=i, slot=slot):
            vector.wait_ge(d_x, pp_dma[i])
            vector.tensor_scalar(xhat[:], xbuf[:, slot, :], 1.0, 0.0, ALU.mult,
                                 ALU.add, accum_out=pp_sum[:, i:i + 1])
            ins = vector.scalar_tensor_tensor(xhat[:], xbuf[:, slot, :], 1.0,
                                              xbuf[:, slot, :], ALU.mult, ALU.mult,
                                              accum_out=pp_ss[:, i:i + 1])
            pp_val[i] = DVE.inc(ins)

    @block.vector
    def _(vector):
        vector.wait_ge(s_dve, pp_val[NT_HALO - 1])
        vector.drain()
        vector.tensor_scalar(m_all[:], pp_sum[:], 1.0 / E, None, ALU.mult)
        vector.tensor_scalar(pp_ss[:], pp_ss[:], 1.0 / E, None, ALU.mult)
        vector.drain()
        vector.tensor_tensor(pp_sum[:], m_all[:], m_all[:], ALU.mult)
        vector.drain()
        ins = vector.tensor_sub(pp_sum[:], pp_ss[:], pp_sum[:])
        DVE.inc(ins)

    @block.scalar
    def _(scalar):
        scalar.wait_ge(s_dve, DVE.n)
        scalar.activation(pp_sum[:], pp_sum[:], AF.Ln, bias=eps_ln[:])
        scalar.drain()
        ins = scalar.activation(rs_all[:], pp_sum[:], AF.Exp, scale=-0.5)
        ACT.inc(ins)
    ln1_ready = ACT.n

    x_dma = [0] * NT_HALO
    tb_dma = [0] * NT_HALO
    a_val = [0] * (NT_HALO + 4)
    b_val = [0] * (NT_HALO + 4)
    c_val = [0] * (NT_HALO + 4)
    ktr_val = [0] * NT_HALO
    trpe_val = [0] * NT_HALO
    kvmm_val = [0] * NT_HALO
    kn_val = [0] * NT_HALO
    kv_done_last = 0

    NIT1 = NT_HALO + 3
    for it in range(NIT1):
        i_dma = it
        i_nrm = it - 1
        i_rp = it - 2
        i_fin = it - 3

        if i_dma < NT_HALO:
            @block.sync
            def _(sync, i=i_dma):
                slot = i % 2
                if i < 2:
                    sync.wait_ge(s_dve, pp_val[NT_HALO - 2 + i])
                if i >= 2:
                    sync.wait_ge(s_dve, a_val[i - 1])
                if i >= 3:
                    sync.wait_ge(s_dve, c_val[i - 1])
                sync.dma_start(out=xbuf[:, slot, :], in_=xh[i * 128:(i + 1) * 128, :]).then_inc(d_x, 16)
                dma["x"] += 16
                x_dma[i] = dma["x"]
                sync.dma_start(out=tbl[:, i % 3, 0, :], in_=cos_in[i * 128:(i + 1) * 128, :]).then_inc(d_tb, 16)
                sync.dma_start(out=tbl[:, i % 3, 1, :], in_=sin_in[i * 128:(i + 1) * 128, :]).then_inc(d_tb, 16)
                dma["tb"] += 32
                tb_dma[i] = dma["tb"]

        @block.vector
        def _(vector, it=it, i_fin=i_fin, i_nrm=i_nrm):
            ops = []
            if 0 <= i_fin < NT_HALO:
                vector.wait_ge(s_act, kn_val[i_fin])
                if i_fin >= 2:
                    vector.wait_ge(s_pe, ktr_val[i_fin - 2])
                rslot = i_fin % 2
                aps = list(apply_ops(rslot, ks_sb[:], qk_bf[:, rslot, :]))
                pslot = i_fin % 2
                vdst = AP(tensor=Vp_sb[:].tensor,
                          offset=Vp_sb[:].offset + i_fin * (KVH * 129),
                          ap=[Vp_sb[:].ap[0], [129, 4], [1, 128]])
                vcopy = lambda v: v.tensor_copy(
                    vdst, ps_kv[pslot][:, 512:1024].rearrange("p (h d) -> p h d", h=4, d=D))
                ops = [aps[0], vcopy, aps[1], aps[2], aps[3]]
            nrm = []
            if 0 <= i_nrm < NT_HALO:
                vector.wait_ge(d_x, x_dma[i_nrm])
                vector.wait_ge(s_act, ln1_ready)
                if i_nrm >= 1:
                    vector.wait_ge(s_pe, trpe_val[i_nrm - 1])
                slot = i_nrm % 2
                nrm = [
                    lambda v, slot=slot, i=i_nrm: v.tensor_scalar(
                        xbuf[:, slot, :], xbuf[:, slot, :], m_all[:, i:i + 1], None, ALU.subtract),
                    lambda v, slot=slot, i=i_nrm: v.tensor_scalar(
                        xhat[:], xbuf[:, slot, :], rs_all[:, i:i + 1], None, ALU.mult),
                ]
            if ops and nrm:
                seq = [ops[0], ops[1], ops[2], nrm[0], ops[3], nrm[1], ops[4]]
            elif ops:
                seq = [ops[0], ops[1], ops[2], None, ops[3], None, ops[4]]
            else:
                seq = [nrm[0], None, nrm[1]] if nrm else []
            ins = None
            for f in seq:
                if f is None:
                    vector.drain()
                else:
                    ins = f(vector)
            if ins is not None:
                a_val[it] = DVE.inc(ins)
            else:
                a_val[it] = DVE.n

        if 0 <= i_fin < NT_HALO:
            @block.tensor
            def _(tensor, it=it, i_fin=i_fin):
                tensor.wait_ge(s_dve, a_val[it])
                rslot = i_fin % 2
                for h in range(4):
                    ins = tensor.transpose(ps_tr[rslot][:, h * 128:(h + 1) * 128],
                                           qk_bf[:, rslot, h * 128:(h + 1) * 128], ident[:])
                ktr_val[i_fin] = PE.inc(ins)

            @block.vector
            def _(vector, it=it, i_fin=i_fin):
                vector.wait_ge(s_pe, ktr_val[i_fin])
                rslot = i_fin % 2
                kdst = AP(tensor=K_sb[:].tensor, offset=K_sb[:].offset + i_fin * 128,
                          ap=[K_sb[:].ap[0], [HALO, 4], [1, 128]])
                ins = vector.tensor_copy(
                    kdst, ps_tr[rslot][:, 0:512].rearrange("p (h d) -> p h d", h=4, d=128))
                b_val[it] = DVE.inc(ins)
        else:
            b_val[it] = DVE.n

        if 0 <= i_nrm < NT_HALO:
            @block.tensor
            def _(tensor, it=it, i_nrm=i_nrm):
                tensor.wait_ge(s_dve, a_val[it])
                tensor.wait_ge(s_dve, b_val[it])
                slot = i_nrm % 2
                for c in range(16):
                    ins = tensor.transpose(ps_tr[slot][:, c * 128:(c + 1) * 128],
                                           xhat[:, c * 128:(c + 1) * 128], ident[:])
                trpe_val[i_nrm] = PE.inc(ins)

        @block.vector
        def _(vector, it=it, i_rp=i_rp, i_nrm=i_nrm):
            ins = None
            rp = []
            if 0 <= i_rp < NT_HALO:
                vector.wait_ge(s_pe, kvmm_val[i_rp])
                vector.wait_ge(d_tb, tb_dma[i_rp])
                rp = list(rope_ops(ps_kv[i_rp % 2][:, 0:512], i_rp % 2, i_rp % 3))
                for f in rp[:3]:
                    ins = f(vector)
            if 0 <= i_nrm < NT_HALO:
                vector.wait_ge(s_pe, trpe_val[i_nrm])
                if i_nrm >= 1:
                    vector.wait_ge(s_pe, kvmm_val[i_nrm - 1])
                ins = vector.tensor_copy(xT[:], ps_tr[i_nrm % 2][:])
            elif rp:
                vector.drain()
            if rp:
                ins = rp[3](vector)
                vector.drain()
                ins = rp[4](vector)
                vector.drain()
                ins = rp[5](vector)
            if ins is not None:
                c_val[it] = DVE.inc(ins)
            else:
                c_val[it] = DVE.n

        if 0 <= i_nrm < NT_HALO:
            @block.tensor
            def _(tensor, it=it, i_nrm=i_nrm):
                tensor.wait_ge(s_dve, c_val[it])
                if i_nrm == 0:
                    tensor.wait_ge(s_act, wkv_ready)
                slot = i_nrm % 2
                for c in range(16):
                    lhsT = xT[:, c * 128:(c + 1) * 128]
                    tensor.matmul(ps_kv[slot][:, 0:512], lhsT, wkv[:, c, 0:512],
                                  start=(c == 0), stop=(c == 15))
                    ins = tensor.matmul(ps_kv[slot][:, 512:1024], lhsT, wkv[:, c, 512:1024],
                                        start=(c == 0), stop=(c == 15))
                kvmm_val[i_nrm] = PE.inc(ins)

        if 0 <= i_rp < NT_HALO:
            @block.scalar
            def _(scalar, it=it, i_rp=i_rp):
                scalar.wait_ge(s_dve, c_val[it])
                rslot = i_rp % 2
                scalar.activation(rp_f[:, rslot, :], rp_ss[:, rslot, :], AF.Ln)
                scalar.drain()
                ins = scalar.activation(rp_f[:, rslot, :], rp_f[:, rslot, :],
                                        AF.Exp, scale=0.5)
                kn_val[i_rp] = ACT.inc(ins)
    kv_done_last = b_val[NIT1 - 1]

    fence()
    st_p1.close()

    if debug:
        @block.sync
        def _(sync):
            sync.dma_start(out=dbg["K"], in_=K_sb[:]).then_inc(d_c, 16)
            sync.dma_start(out=dbg["V"], in_=Vp_sb[:]).then_inc(d_c, 16)
            dma["c"] += 32
            sync.wait_ge(d_c, dma["c"])
    if upto == "p1":
        st_p12.close()
        st_kvq.close()
        st_top.close()
        return nc

    fence()
    e2 = st_p2.enter_context
    wqh = e2(nc.sbuf_tensor("wqh", [128, 16, 1024], BF16))
    qs_sb = e2(nc.sbuf_tensor("qs_sb", [128, 2, 512], F32))
    ps_q = [e2(nc.psum_tensor(f"ps_q{k}", [128, 1024], F32)) for k in range(2)]
    wq_r = wq_in.rearrange("(c p) f -> c p f", p=128)

    for g in range(2):
        if g == 1:
            fence()
        pieces = []
        for c in range(16):
            for hcol in range(2):
                pieces.append((wq_r[c][:, g * 1024 + hcol * 512:g * 1024 + (hcol + 1) * 512],
                               wqh[:, c, hcol * 512:(hcol + 1) * 512]))
        n0 = len(stage_hist)
        stream_weight(pieces)
        wq_vals = stage_hist[n0:]

        @block.sync
        def _(sync, g=g):
            for sg in range(2):
                sync.dma_start(out=qs_sb[:, sg, :],
                               in_=qs_in[:, (g * 2 + sg) * 512:(g * 2 + sg + 1) * 512]).then_inc(d_c, 16)
                dma["c"] += 16
        qs_dma = dma["c"]

        x_dma2 = [0] * NT_OWN
        tb_dma2 = [0] * NT_OWN
        a2 = [0] * (NT_OWN + 4)
        b2 = [0] * (NT_OWN + 4)
        c2 = [0] * (NT_OWN + 4)
        xv = [0] * (NT_OWN + 4)
        qtr2 = [0] * NT_OWN
        trpe2 = [0] * NT_OWN
        qmm2 = [0] * NT_OWN
        kn2 = [0] * NT_OWN

        NIT2 = NT_OWN + 3
        for it in range(NIT2):
            i_dma = it
            i_nrm = it - 1
            i_rp = it - 2
            i_fin = it - 3

            if i_dma < NT_OWN:
                @block.sync
                def _(sync, i=i_dma, g=g):
                    slot = i % 2
                    hi = i + 8
                    if i >= 2:
                        sync.wait_ge(s_dve, a2[i - 1])
                    if i >= 3:
                        sync.wait_ge(s_dve, c2[i - 1])
                    sync.dma_start(out=xbuf[:, slot, :], in_=xh[hi * 128:(hi + 1) * 128, :]).then_inc(d_x, 16)
                    dma["x"] += 16
                    x_dma2[i] = dma["x"]
                    sync.dma_start(out=tbl[:, i % 3, 0, :], in_=cos_in[hi * 128:(hi + 1) * 128, :]).then_inc(d_tb, 16)
                    sync.dma_start(out=tbl[:, i % 3, 1, :], in_=sin_in[hi * 128:(hi + 1) * 128, :]).then_inc(d_tb, 16)
                    dma["tb"] += 32
                    tb_dma2[i] = dma["tb"]

            @block.vector
            def _(vector, it=it, i_fin=i_fin, i_nrm=i_nrm, g=g):
                ap0 = ap1 = None
                if 0 <= i_fin < NT_OWN:
                    vector.wait_ge(s_act, kn2[i_fin])
                    vector.wait_ge(d_c, qs_dma)
                    if i_fin >= 1:
                        vector.wait_ge(s_pe, qtr2[i_fin - 1])
                    ap0 = list(apply_ops(0, qs_sb[:, 0, :], qk_bf[:, 0, :]))
                    ap1 = list(apply_ops(1, qs_sb[:, 1, :], qk_bf[:, 1, :]))
                nrm = []
                if 0 <= i_nrm < NT_OWN:
                    vector.wait_ge(d_x, x_dma2[i_nrm])
                    if i_nrm >= 1:
                        vector.wait_ge(s_pe, trpe2[i_nrm - 1])
                    slot = i_nrm % 2
                    hi = i_nrm + 8
                    nrm = [
                        lambda v, slot=slot, hi=hi: v.tensor_scalar(
                            xbuf[:, slot, :], xbuf[:, slot, :], m_all[:, hi:hi + 1], None, ALU.subtract),
                        lambda v, slot=slot, hi=hi: v.tensor_scalar(
                            xhat[:], xbuf[:, slot, :], rs_all[:, hi:hi + 1], None, ALU.mult),
                    ]
                ins = None
                if ap0 and nrm:
                    seq = [ap0[0], ap1[0], nrm[0], ap0[1], ap1[1], nrm[1],
                           ap0[2], ap1[2], ap0[3], ap1[3]]
                elif ap0:
                    seq = [ap0[0], ap1[0], None, ap0[1], ap1[1], None,
                           ap0[2], ap1[2], ap0[3], ap1[3]]
                else:
                    seq = [nrm[0], None, nrm[1]] if nrm else []
                for f in seq:
                    if f is None:
                        vector.drain()
                    else:
                        ins = f(vector)
                a2[it] = DVE.inc(ins) if ins is not None else DVE.n

            if 0 <= i_fin < NT_OWN:
                @block.tensor
                def _(tensor, it=it, i_fin=i_fin):
                    tensor.wait_ge(s_dve, a2[it])
                    if it >= 1:
                        tensor.wait_ge(s_act, b2[it - 1])
                    for sg in range(2):
                        for h in range(4):
                            ins = tensor.transpose(
                                ps_tr[0][:, (sg * 4 + h) * 128:(sg * 4 + h + 1) * 128],
                                qk_bf[:, sg, h * 128:(h + 1) * 128], ident[:])
                    qtr2[i_fin] = PE.inc(ins)

            if 0 <= i_nrm < NT_OWN:
                @block.tensor
                def _(tensor, it=it, i_nrm=i_nrm):
                    tensor.wait_ge(s_dve, a2[it])
                    if i_nrm >= 1:
                        tensor.wait_ge(s_act, xv[it - 1])
                    for c in range(16):
                        ins = tensor.transpose(ps_tr[1][:, c * 128:(c + 1) * 128],
                                               xhat[:, c * 128:(c + 1) * 128], ident[:])
                    trpe2[i_nrm] = PE.inc(ins)

            if 0 <= i_fin < NT_OWN or 0 <= i_nrm < NT_OWN:
                @block.scalar
                def _(scalar, it=it, i_fin=i_fin, i_nrm=i_nrm, g=g):
                    if 0 <= i_fin < NT_OWN:
                        scalar.wait_ge(s_pe, qtr2[i_fin])
                        ins = None
                        for sg in range(2):
                            h0 = g * 8 + sg * 4
                            qdst = AP(tensor=Qfm[:].tensor,
                                      offset=Qfm[:].offset + h0 * TOWN + i_fin * 128,
                                      ap=[Qfm[:].ap[0], [TOWN, 4], [1, 128]])
                            ins = scalar.activation(
                                qdst, ps_tr[0][:, sg * 512:(sg + 1) * 512].rearrange("p (h d) -> p h d", h=4, d=128),
                                AF.Copy)
                        b2[it] = ACT.inc(ins)
                    else:
                        b2[it] = ACT.n
                    if 0 <= i_nrm < NT_OWN:
                        scalar.wait_ge(s_pe, trpe2[i_nrm])
                        if i_nrm >= 1:
                            scalar.wait_ge(s_pe, qmm2[i_nrm - 1])
                        scalar.activation(xT[:], ps_tr[1][:], AF.Copy)
                        ins = scalar.activation(xT[:, 0:128], ps_tr[1][:, 0:128], AF.Copy)
                        xv[it] = ACT.inc(ins)
                    else:
                        xv[it] = ACT.n
            else:
                b2[it] = ACT.n
                xv[it] = ACT.n

            @block.vector
            def _(vector, it=it, i_rp=i_rp, i_nrm=i_nrm):
                ins = None
                rp0 = rp1 = None
                if 0 <= i_rp < NT_OWN:
                    vector.wait_ge(s_pe, qmm2[i_rp])
                    vector.wait_ge(d_tb, tb_dma2[i_rp])
                    slot = i_rp % 2
                    rp0 = list(rope_ops(ps_q[slot][:, 0:512], 0, i_rp % 3))
                    rp1 = list(rope_ops(ps_q[slot][:, 512:1024], 1, i_rp % 3))
                    for k in range(3):
                        ins = rp0[k](vector)
                        ins = rp1[k](vector)
                    vector.drain()
                    ins = rp0[3](vector)
                    ins = rp1[3](vector)
                    ins = rp0[4](vector)
                    ins = rp1[4](vector)
                    ins = rp0[5](vector)
                    ins = rp1[5](vector)
                c2[it] = DVE.inc(ins) if ins is not None else DVE.n

            if 0 <= i_nrm < NT_OWN:
                @block.tensor
                def _(tensor, it=it, i_nrm=i_nrm, g=g):
                    tensor.wait_ge(s_act, xv[it])
                    if it >= 1:
                        tensor.wait_ge(s_dve, c2[it - 1])
                    slot = i_nrm % 2
                    for c in range(16):
                        if i_nrm == 0:
                            tensor.wait_ge(s_act, wq_vals[2 * c + 1])
                        lhsT = xT[:, c * 128:(c + 1) * 128]
                        tensor.matmul(ps_q[slot][:, 0:512], lhsT, wqh[:, c, 0:512],
                                      start=(c == 0), stop=(c == 15))
                        ins = tensor.matmul(ps_q[slot][:, 512:1024], lhsT, wqh[:, c, 512:1024],
                                            start=(c == 0), stop=(c == 15))
                    qmm2[i_nrm] = PE.inc(ins)

            if 0 <= i_rp < NT_OWN:
                @block.scalar
                def _(scalar, it=it, i_rp=i_rp):
                    scalar.wait_ge(s_dve, c2[it])
                    scalar.activation(rp_f[:, 0, :], rp_ss[:, 0, :], AF.Ln)
                    scalar.activation(rp_f[:, 1, :], rp_ss[:, 1, :], AF.Ln)
                    scalar.drain()
                    scalar.activation(rp_f[:, 0, :], rp_f[:, 0, :], AF.Exp, scale=0.5)
                    ins = scalar.activation(rp_f[:, 1, :], rp_f[:, 1, :], AF.Exp, scale=0.5)
                    kn2[i_rp] = ACT.inc(ins)

    fence()
    st_p2.close()
    st_p12.close()

    if debug:
        @block.sync
        def _(sync):
            sync.dma_start(out=dbg["Q"], in_=Qfm[:]).then_inc(d_c, 16)
            dma["c"] += 16
            sync.wait_ge(d_c, dma["c"])
    if upto == "p2":
        st_kvq.close()
        st_top.close()
        return nc

    e3 = st_p3.enter_context
    maskL = e3(nc.sbuf_tensor("maskL", [128, 512], BF16))
    maskR = e3(nc.sbuf_tensor("maskR", [128, 512], BF16))
    vb_sb = e3(nc.sbuf_tensor("vb_sb", [128, NT_OWN * NCH], F32))
    pt_ring = e3(nc.sbuf_tensor("pt_ring", [128, 12, 1024], BF16))
    otm = e3(nc.sbuf_tensor("otm", [128, 512], BF16))
    orc = e3(nc.sbuf_tensor("orc", [128, 4], F32))
    ofm = e3(nc.sbuf_tensor("ofm", [128, 2, H * 128], BF16))
    ps_st = [e3(nc.psum_tensor(f"ps_st{k}", [128, 1024], F32)) for k in range(2)]
    ps_pv = [e3(nc.psum_tensor(f"ps_pv{k}", [128, 258], F32)) for k in range(2)]
    ps_ot = e3(nc.psum_tensor("ps_ot", [128, 512], BF16))

    @block.sync
    def _(sync):
        for dst, src in [(maskL, maskL_h), (maskR, maskR_h), (vb_sb, vb_in)]:
            sync.dma_start(out=dst[:], in_=src).then_inc(d_c, 16)
            dma["c"] += 16
    masks_dma = dma["c"]

    def _emit_norm_tail(itpair):
        qt, kv = itpair

        @block.vector
        def _(vector, qt=qt, kv=kv):
            vector.wait_ge(s_pe, PE.n)
            vector.wait_ge(s_pe, st3["otm_pe"])
            for jj in range(2):
                t = ps_pv[jj][:]
                dsrc = AP(tensor=t.tensor, offset=t.offset + 128,
                          ap=[t.ap[0], [129, 2], [1, 1]])
                vector.tensor_copy(orc[:, jj * 2:(jj + 1) * 2], dsrc)
            vector.drain()
            vector.reciprocal(orc[:], orc[:])
            vector.drain()
            for jj in range(2):
                t = ps_pv[jj][:]
                osrc = AP(tensor=t.tensor, offset=t.offset,
                          ap=[t.ap[0], [129, 2], [1, 128]])
                rb = AP(tensor=orc[:].tensor, offset=orc[:].offset + jj * 2,
                        ap=[orc[:].ap[0], [1, 2], [0, 128]])
                ins = vector.tensor_tensor(
                    otm[:, jj * 256:(jj + 1) * 256].rearrange("p (h d) -> p h d", h=2, d=128),
                    osrc, rb, ALU.mult)
            DVE.inc(ins)
        st3["pv_norm"] = DVE.n

        @block.tensor
        def _(tensor):
            tensor.wait_ge(s_dve, st3["pv_norm"])
            tensor.wait_ge(s_dve, st3["ot_evac"])
            for hh in range(4):
                ins = tensor.transpose(ps_ot[:, hh * 128:(hh + 1) * 128],
                                       otm[:, hh * 128:(hh + 1) * 128], ident[:])
            PE.inc(ins)
        st3["otm_pe"] = PE.n

        @block.vector
        def _(vector, qt=qt, kv=kv):
            vector.wait_ge(s_pe, st3["otm_pe"])
            if kv == 0:
                waits(vector, g_ofm[qt % 2])
            ins = vector.tensor_copy(ofm[:, qt % 2, kv * 512:(kv + 1) * 512], ps_ot[:])
            DVE.inc(ins)
        st3["ot_evac"] = DVE.n

        if kv == KVH - 1:
            @block.sync
            def _(sync, qt=qt):
                sync.wait_ge(s_dve, st3["ot_evac"])
                sync.dma_start(out=scr_o[qt], in_=ofm[:, qt % 2, :]).then_inc(d_sc, 16)
                dma["sc"] += 16
            ofm_dma_val[qt] = dma["sc"]
            g_ofm[qt % 2] = [(d_sc, dma["sc"])]

    iters = [(qt, kv) for qt in range(NT_OWN) for kv in range(KVH)]
    NPT = 12
    st3 = {"pv_norm": 0, "otm_pe": 0, "ot_evac": 0}
    rec_exp = {}
    rec_stpe = {}
    rec_pvpe = {}
    rec_mask = {}
    ofm_dma_val = [0] * NT_OWN
    g_ofm = [[], []]

    def pcs(p):
        return [2 * p] + ([2 * p + 1] if 2 * p + 1 < NCH else [])

    act0 = ACT.n
    for it, (qt, kv) in enumerate(iters):
        act_base = act0 + it * NPAIR

        @block.tensor
        def _(tensor, it=it, qt=qt, kv=kv, act_base=act_base):
            if it == 0:
                tensor.wait_ge(d_c, masks_dma)
            pqt, pkv = iters[it - 1] if it >= 1 else (None, None)
            for p in range(NPAIR):
                g = it * NPAIR + p
                sslot = g % 2
                if p >= 2:
                    tensor.wait_ge(s_act, act_base + p - 1)
                elif it >= 1:
                    tensor.wait_ge(s_act, rec_exp[(it - 1, NPAIR - 2 + p)])
                for c in pcs(p):
                    lhsT = K_sb[:, kv, (qt + c) * 128:(qt + c + 1) * 128]
                    rhs = AP(tensor=Qfm[:].tensor,
                             offset=Qfm[:].offset + (4 * kv) * TOWN + qt * 128,
                             ap=[Qfm[:].ap[0], [TOWN, 4], [1, 128]])
                    ins = tensor.matmul(ps_st[sslot][:, (c % 2) * 512:(c % 2 + 1) * 512],
                                        lhsT, rhs, start=True, stop=True)
                rec_stpe[(it, p)] = PE.inc(ins)
                if it >= 1:
                    pg = (it - 1) * NPAIR + p
                    pslot = pg % NPT
                    for c, req in rec_mask[(it - 1, p)]:
                        tensor.wait_ge(s_act, req["act"])
                        if "dve" in req:
                            tensor.wait_ge(s_dve, req["dve"])
                        if c == 0:
                            tensor.wait_ge(s_dve, st3["pv_norm"])
                        col = (c % 2) * 512
                        for hh in range(4):
                            lhsT = pt_ring[:, pslot, col + hh * 128:col + (hh + 1) * 128]
                            rhs = Vp_sb[:, pqt + c, pkv, :]
                            ins = tensor.matmul(
                                ps_pv[hh // 2][:, (hh % 2) * 129:(hh % 2 + 1) * 129],
                                lhsT, rhs, start=(c == 0 and hh % 2 == 0),
                                stop=(c == NCH - 1))
                    rec_pvpe[(it - 1, p)] = PE.inc(ins)

        if it >= 1:
            _emit_norm_tail(iters[it - 1])

        for p in range(NPAIR):
            rec_exp[(it, p)] = act_base + p + 1

        @block.vector
        def _(vector, it=it, qt=qt):
            for p in range(NPAIR):
                g = it * NPAIR + p
                pslot = g % NPT
                auth = []
                for c in pcs(p):
                    tri, needs_vb = _mask_kind(qt, c)
                    col = (c % 2) * 512
                    pa = pt_ring[:, pslot, col:col + 512]
                    vbcol = vb_sb[:, qt * NCH + c:qt * NCH + c + 1]
                    if tri is not None or needs_vb:
                        vector.wait_ge(s_act, rec_exp[(it, p)])
                        if tri is not None and needs_vb:
                            m = maskL[:] if tri == "L" else maskR[:]
                            ins = vector.scalar_tensor_tensor(pa, pa, vbcol, m,
                                                              ALU.mult, ALU.mult)
                        elif tri is not None:
                            m = maskL[:] if tri == "L" else maskR[:]
                            ins = vector.tensor_tensor(pa, pa, m, ALU.mult)
                        else:
                            ins = vector.tensor_scalar(pa, pa, vbcol, None, ALU.mult)
                        auth.append((c, {"act": rec_exp[(it, p)], "dve": DVE.inc(ins)}))
                    else:
                        auth.append((c, {"act": rec_exp[(it, p)]}))
                rec_mask[(it, p)] = auth

    @block.scalar
    def _(scalar):
        for it2 in range(len(iters)):
            for p in range(NPAIR):
                g = it2 * NPAIR + p
                sslot = g % 2
                pslot = g % NPT
                scalar.wait_ge(s_pe, rec_stpe[(it2, p)])
                og, rem = divmod(g - NPT, NPAIR)
                if g >= NPT and (og, rem) in rec_pvpe:
                    scalar.wait_ge(s_pe, rec_pvpe[(og, rem)])
                ncol = len(pcs(p)) * 512
                scalar.activation(ps_st[sslot][:, 0:ncol], ps_st[sslot][:, 0:ncol],
                                  AF.Tanh, scale=1.0 / (CAP * float(np.sqrt(float(D)))))
                ins = scalar.activation(pt_ring[:, pslot, 0:ncol],
                                        ps_st[sslot][:, 0:ncol], AF.Exp, scale=CAP)
                v = ACT.inc(ins)
                assert v == rec_exp[(it2, p)]

    it = len(iters)

    @block.tensor
    def _(tensor):
        pqt, pkv = iters[-1]
        for p in range(NPAIR):
            pg = (it - 1) * NPAIR + p
            pslot = pg % NPT
            for c, req in rec_mask[(it - 1, p)]:
                tensor.wait_ge(s_act, req["act"])
                if "dve" in req:
                    tensor.wait_ge(s_dve, req["dve"])
                if c == 0:
                    tensor.wait_ge(s_dve, st3["pv_norm"])
                col = (c % 2) * 512
                for hh in range(4):
                    lhsT = pt_ring[:, pslot, col + hh * 128:col + (hh + 1) * 128]
                    rhs = Vp_sb[:, pqt + c, pkv, :]
                    ins = tensor.matmul(
                        ps_pv[hh // 2][:, (hh % 2) * 129:(hh % 2 + 1) * 129],
                        lhsT, rhs, start=(c == 0 and hh % 2 == 0),
                        stop=(c == NCH - 1))
            rec_pvpe[(it - 1, p)] = PE.inc(ins)

    _emit_norm_tail(iters[-1])

    fence()
    st_p3.close()
    st_kvq.close()

    if debug:
        for qt in range(NT_OWN):
            @block.sync
            def _(sync, qt=qt):
                sync.dma_start(out=dbg["O"][qt], in_=scr_o[qt]).then_inc(d_c, 16)
                dma["c"] += 16
        @block.sync
        def _(sync):
            sync.wait_ge(d_c, dma["c"])
    if upto == "p3":
        st_top.close()
        return nc

    scr_y1 = nc.dram_tensor("scr_y1", [NT_OWN, 128, E], F32, kind="Internal").ap()
    rs2 = en(nc.sbuf_tensor("rs2", [128, NT_OWN], F32))
    m2_ = en(nc.sbuf_tensor("m2_", [128, NT_OWN], F32))
    var2 = en(nc.sbuf_tensor("var2", [128, NT_OWN], F32))
    msq2 = en(nc.sbuf_tensor("msq2", [128, NT_OWN], F32))

    stW = ExitStack()
    eW = stW.enter_context
    wbra_sb = eW(nc.sbuf_tensor("wbra_sb", [128, 16, 2120], BF16))
    bflat_sb = eW(nc.sbuf_tensor("bflat_sb", [64, 2048], BF16))
    bb_stage = eW(nc.sbuf_tensor("bb_stage", [1, E], F32))
    bb_bf = eW(nc.sbuf_tensor("bb_bf", [1, E], BF16))
    onesc = eW(nc.sbuf_tensor("onesc", [1, 128], BF16))
    wstg45 = eW(nc.sbuf_tensor("wstg45", [128, 3, 1024], F32))

    stA = ExitStack()
    eA = stA.enter_context
    wo_sb = eA(nc.sbuf_tensor("wo_sb", [128, 16, 2048], BF16))
    ot_sb = eA(nc.sbuf_tensor("ot_sb", [128, 2, H * 128], BF16))
    y1b = eA(nc.sbuf_tensor("y1b", [128, 2, E], F32))
    stA_sum = eA(nc.sbuf_tensor("stA_sum", [128, NT_OWN], F32))
    stA_ss = eA(nc.sbuf_tensor("stA_ss", [128, NT_OWN], F32))
    ps_wo = [eA(nc.psum_tensor(f"ps_wo{k}", [128, E], F32)) for k in range(2)]

    wo_r = wo_in.rearrange("(c p) f -> c p f", p=128)
    pieces = []
    for c in range(16):
        for n in range(2):
            pieces.append((wo_r[c][:, n * 1024:(n + 1) * 1024], wo_sb[:, c, n * 1024:(n + 1) * 1024]))
    n0 = len(stage_hist)
    stream_weight(pieces, staging=wstg45, nslots=3)
    wo_vals = stage_hist[n0:]

    @block.sync
    def _(sync):
        for dst, src in [(bb_stage, bb_in), (onesc, ones_h)]:
            sync.dma_start(out=dst[:], in_=src).then_inc(d_c, 16)
            dma["c"] += 16
    bbdma = dma["c"]

    @block.scalar
    def _(scalar):
        scalar.wait_ge(d_c, bbdma)
        ins = scalar.activation(bb_bf[:], bb_stage[:], AF.Copy)
        ACT.inc(ins)
    bb_ready = ACT.n

    wbra_r = wbra_in.rearrange("(c p) f -> c p f", p=128)
    pieces = []
    for c in range(16):
        for n in range(2):
            pieces.append((wbra_r[c][:, n * 1024:(n + 1) * 1024], wbra_sb[:, c, n * 1024:(n + 1) * 1024]))
        pieces.append((wbra_r[c][:, 2048:2120], wbra_sb[:, c, 2048:2120]))
    for n in range(2):
        pieces.append((bflat_in[:, n * 1024:(n + 1) * 1024], bflat_sb[:, n * 1024:(n + 1) * 1024]))
    n0 = len(stage_hist)
    stream_weight(pieces, staging=wstg45, nslots=3)
    ffn_ready = stage_hist[-1]

    woA = [0] * NT_OWN
    y1A = [0] * NT_OWN
    y1_dma = [0] * NT_OWN
    o_dma = [0] * NT_OWN
    x_dmaA = [0] * NT_OWN
    rsg_val = [0] * (NT_OWN // 4)

    for i in range(NT_OWN):
        slot = i % 2
        hi = i + 8

        @block.sync
        def _(sync, i=i, slot=slot, hi=hi):
            if i >= 2:
                sync.wait_ge(s_dve, y1A[i - 2])
            sync.wait_ge(d_sc, ofm_dma_val[i])
            sync.dma_start(out=ot_sb[:, slot, :], in_=scr_o[i]).then_inc(d_sc, 16)
            dma["sc"] += 16
            o_dma[i] = dma["sc"]
            sync.dma_start(out=xbuf[:, slot, :], in_=xh[hi * 128:(hi + 1) * 128, :]).then_inc(d_x, 16)
            dma["x"] += 16
            x_dmaA[i] = dma["x"]

        @block.tensor
        def _(tensor, i=i, slot=slot):
            tensor.wait_ge(d_sc, o_dma[i])
            if i >= 2:
                tensor.wait_ge(s_dve, y1A[i - 2])
            for c in range(16):
                if i == 0:
                    tensor.wait_ge(s_act, wo_vals[2 * c + 1])
                lhsT = ot_sb[:, slot, c * 128:(c + 1) * 128]
                for n in range(4):
                    ins = tensor.matmul(ps_wo[slot][:, n * 512:(n + 1) * 512], lhsT,
                                        wo_sb[:, c, n * 512:(n + 1) * 512],
                                        start=(c == 0), stop=(c == 15))
            woA[i] = PE.inc(ins)

        @block.vector
        def _(vector, i=i, slot=slot):
            vector.wait_ge(s_pe, woA[i])
            vector.wait_ge(d_x, x_dmaA[i])
            vector.scalar_tensor_tensor(y1b[:, slot, :], ps_wo[slot][:], 1.0,
                                        xbuf[:, slot, :], ALU.mult, ALU.add,
                                        accum_out=stA_sum[:, i:i + 1])
            vector.drain()
            ins = vector.scalar_tensor_tensor(xhat[:], y1b[:, slot, :], 1.0,
                                              y1b[:, slot, :], ALU.mult, ALU.mult,
                                              accum_out=stA_ss[:, i:i + 1])
            y1A[i] = DVE.inc(ins)

        @block.sync
        def _(sync, i=i, slot=slot):
            sync.wait_ge(s_dve, y1A[i])
            sync.dma_start(out=scr_y1[i], in_=y1b[:, slot, :]).then_inc(d_y1, 16)
            dma["y1"] += 16
            y1_dma[i] = dma["y1"]

        if i % 4 == 3:
            g = i // 4

            @block.vector
            def _(vector, g=g, i=i):
                vector.wait_ge(s_dve, y1A[i])
                vector.drain()
                vector.tensor_scalar(m2_[:, g * 4:g * 4 + 4], stA_sum[:, g * 4:g * 4 + 4],
                                     1.0 / E, None, ALU.mult)
                vector.drain()
                vector.tensor_tensor(msq2[:, g * 4:g * 4 + 4], m2_[:, g * 4:g * 4 + 4],
                                     m2_[:, g * 4:g * 4 + 4], ALU.mult)
                vector.drain()
                ins = vector.scalar_tensor_tensor(var2[:, g * 4:g * 4 + 4],
                                                  stA_ss[:, g * 4:g * 4 + 4], 1.0 / E,
                                                  msq2[:, g * 4:g * 4 + 4],
                                                  ALU.mult, ALU.subtract)
                DVE.inc(ins)

            @block.scalar
            def _(scalar, g=g):
                scalar.wait_ge(s_dve, DVE.n)
                scalar.activation(var2[:, g * 4:g * 4 + 4], var2[:, g * 4:g * 4 + 4],
                                  AF.Ln, bias=eps_ln[:])
                scalar.drain()
                ins = scalar.activation(rs2[:, g * 4:g * 4 + 4], var2[:, g * 4:g * 4 + 4],
                                        AF.Exp, scale=-0.5)
                rsg_val[g] = ACT.inc(ins)

    a_done_pe = woA[NT_OWN - 1]
    a_done_dve = y1A[NT_OWN - 1]
    a_done_dma = y1_dma[NT_OWN - 1]
    stA.close()

    stB = ExitStack()
    eB = stB.enter_context
    y1c = eB(nc.sbuf_tensor("y1c", [128, 3, E], F32))
    xT2 = eB(nc.sbuf_tensor("xT2", [128, 2, E], BF16))
    hpr2 = eB(nc.sbuf_tensor("hpr2", [128, 2, 64], BF16))
    lgha = eB(nc.sbuf_tensor("lgha", [128, 72], F32))
    lg = eB(nc.sbuf_tensor("lg", [128, 8], F32))
    gm1 = eB(nc.sbuf_tensor("gm1", [128, 1], F32))
    gnm1 = eB(nc.sbuf_tensor("gnm1", [128, 1], F32))
    gm2 = eB(nc.sbuf_tensor("gm2", [128, 1], F32))
    gtmp = eB(nc.sbuf_tensor("gtmp", [128, 8], F32))
    gwex = eB(nc.sbuf_tensor("gwex", [128, 8], F32))
    gsum = eB(nc.sbuf_tensor("gsum", [128, 1], F32))
    hT = eB(nc.sbuf_tensor("hT", [64, 128], BF16))
    ps_ffn = eB(nc.psum_tensor("ps_ffn", [128, E], F32))
    ps_trB = eB(nc.psum_tensor("ps_trB", [128, E], BF16))
    ps_ra = eB(nc.psum_tensor("ps_ra", [128, 72], F32))
    ps_hT = eB(nc.psum_tensor("ps_hT", [64, 128], BF16))

    ffn_done = [0] * NT_OWN
    out_dma = [0] * NT_OWN
    raB = [0] * NT_OWN
    ra_ev = [0] * NT_OWN
    normB = [0] * NT_OWN
    trB = [0] * NT_OWN
    xTB = [0] * NT_OWN
    mmB = [0] * NT_OWN
    gateB = [0] * NT_OWN
    hTpe = [0] * NT_OWN
    hTev = [0] * NT_OWN
    y1_dma2 = [0] * NT_OWN

    def emit_lagged(j):
        slot = j % 2
        slot3 = j % 3

        @block.tensor
        def _(tensor, j=j, slot=slot):
            tensor.wait_ge(s_dve, gateB[j])
            if j >= 1:
                tensor.wait_ge(s_dve, hTev[j - 1])
            ins = tensor.transpose(ps_hT[:], hpr2[:, slot, :], ident[:])
            hTpe[j] = PE.inc(ins)

        @block.vector
        def _(vector, j=j):
            vector.wait_ge(s_pe, hTpe[j])
            ins = vector.tensor_copy(hT[:], ps_hT[:])
            hTev[j] = DVE.inc(ins)

        @block.tensor
        def _(tensor, j=j, slot=slot):
            if j == 0:
                tensor.wait_ge(s_act, bb_ready)
            if j >= 1:
                tensor.wait_ge(s_dve, ffn_done[j - 1])
            for n in range(4):
                ins = tensor.matmul(ps_ffn[:, n * 512:(n + 1) * 512], onesc[:],
                                    bb_bf[:, n * 512:(n + 1) * 512],
                                    start=True, stop=False)
            for c in range(16):
                lhsT = xT2[:, slot, c * 128:(c + 1) * 128]
                for n in range(4):
                    ins = tensor.matmul(ps_ffn[:, n * 512:(n + 1) * 512], lhsT,
                                        wbra_sb[:, c, n * 512:(n + 1) * 512],
                                        start=False, stop=False)
            tensor.wait_ge(s_dve, hTev[j])
            for n in range(4):
                ins = tensor.matmul(ps_ffn[:, n * 512:(n + 1) * 512], hT[:],
                                    bflat_sb[:, n * 512:(n + 1) * 512],
                                    start=False, stop=True)
            mmB[j] = PE.inc(ins)

        @block.vector
        def _(vector, j=j, slot3=slot3):
            vector.wait_ge(s_pe, mmB[j])
            ins = vector.tensor_add(y1c[:, slot3, :], y1c[:, slot3, :], ps_ffn[:])
            ffn_done[j] = DVE.inc(ins)

        @block.gpsimd
        def _(gpsimd, j=j, slot3=slot3):
            gpsimd.wait_ge(s_dve, ffn_done[j])
            gpsimd.dma_start(out=out[j * 128:(j + 1) * 128, :], in_=y1c[:, slot3, :]).then_inc(d_out, 16)
            dma["out"] += 16
            out_dma[j] = dma["out"]

    for i in range(NT_OWN):
        slot = i % 2
        slot3 = i % 3

        @block.sync
        def _(sync, i=i, slot3=slot3):
            if i < 2:
                sync.wait_ge(s_pe, a_done_pe)
                sync.wait_ge(s_dve, a_done_dve)
                sync.wait_ge(d_y1, a_done_dma)
                sync.wait_ge(s_act, rsg_val[NT_OWN // 4 - 1])
            if i >= 3:
                sync.wait_ge(s_dve, ffn_done[i - 3])
                sync.wait_ge(d_out, out_dma[i - 3])
            sync.wait_ge(d_y1, y1_dma[i])
            sync.dma_start(out=y1c[:, slot3, :], in_=scr_y1[i]).then_inc(d_sc, 16)
            dma["sc"] += 16
            y1_dma2[i] = dma["sc"]

        @block.vector
        def _(vector, i=i, slot3=slot3):
            vector.wait_ge(d_sc, y1_dma2[i])
            vector.wait_ge(s_act, rsg_val[i // 4])
            if i >= 1:
                vector.wait_ge(s_pe, trB[i - 1])
            rsb = AP(tensor=rs2[:].tensor, offset=rs2[:].offset + i,
                     ap=[rs2[:].ap[0], [0, E]])
            ins = vector.scalar_tensor_tensor(xhat[:], y1c[:, slot3, :],
                                              m2_[:, i:i + 1], rsb,
                                              ALU.subtract, ALU.mult)
            normB[i] = DVE.inc(ins)

        @block.tensor
        def _(tensor, i=i):
            tensor.wait_ge(s_dve, normB[i])
            if i >= 1:
                tensor.wait_ge(s_dve, xTB[i - 1])
            for c in range(16):
                ins = tensor.transpose(ps_trB[:, c * 128:(c + 1) * 128],
                                       xhat[:, c * 128:(c + 1) * 128], ident[:])
            trB[i] = PE.inc(ins)

        @block.vector
        def _(vector, i=i, slot=slot):
            vector.wait_ge(s_pe, trB[i])
            if i >= 2:
                vector.wait_ge(s_pe, mmB[i - 2])
            ins = vector.tensor_copy(xT2[:, slot, :], ps_trB[:])
            xTB[i] = DVE.inc(ins)

        @block.tensor
        def _(tensor, i=i, slot=slot):
            tensor.wait_ge(s_dve, xTB[i])
            if i == 0:
                tensor.wait_ge(s_act, ffn_ready)
            if i >= 1:
                tensor.wait_ge(s_dve, ra_ev[i - 1])
            for c in range(16):
                ins = tensor.matmul(ps_ra[:], xT2[:, slot, c * 128:(c + 1) * 128],
                                    wbra_sb[:, c, 2048:2120],
                                    start=(c == 0), stop=(c == 15))
            raB[i] = PE.inc(ins)

        @block.vector
        def _(vector, i=i):
            vector.wait_ge(s_pe, raB[i])
            ins = vector.tensor_copy(lgha[:], ps_ra[:])
            ra_ev[i] = DVE.inc(ins)
            vector.drain()
            vector.tensor_copy(lg[:], lgha[:, 0:8])
            vector.drain()
            vector.tensor_reduce(gm1[:], lg[:], AX.X, ALU.max)
            vector.drain()
            vector.tensor_scalar(gtmp[:], lg[:], gm1[:], -1e30, ALU.is_ge, ALU.mult)
            vector.drain()
            vector.tensor_add(gtmp[:], gtmp[:], lg[:])
            vector.drain()
            vector.tensor_reduce(gm2[:], gtmp[:], AX.X, ALU.max)
            vector.drain()
            ins = vector.tensor_scalar(gnm1[:], gm1[:], -1.0, None, ALU.mult)
            DVE.inc(ins)

        @block.scalar
        def _(scalar):
            scalar.wait_ge(s_dve, DVE.n)
            ins = scalar.activation(gwex[:], lg[:], AF.Exp, bias=gnm1[:])
            ACT.inc(ins)

        @block.vector
        def _(vector, i=i, slot=slot):
            vector.wait_ge(s_act, ACT.n)
            vector.tensor_scalar(gtmp[:], lg[:], gm2[:], None, ALU.is_ge)
            vector.drain()
            vector.tensor_mul(gwex[:], gwex[:], gtmp[:])
            vector.drain()
            vector.tensor_reduce(gsum[:], gwex[:], AX.X, ALU.add)
            vector.drain()
            vector.reciprocal(gsum[:], gsum[:])
            vector.drain()
            vector.tensor_scalar(gsum[:], gsum[:], SCALE_MOE, None, ALU.mult)
            vector.drain()
            vector.tensor_scalar(gwex[:], gwex[:], gsum[:], None, ALU.mult)
            vector.drain()
            if i >= 2:
                vector.wait_ge(s_pe, hTpe[i - 2])
            wb = AP(tensor=gwex[:].tensor, offset=gwex[:].offset,
                    ap=[gwex[:].ap[0], [1, 8], [0, 8]])
            ins = vector.tensor_tensor(hpr2[:, slot, :].rearrange("p (e r) -> p e r", e=8),
                                       lgha[:, 8:72].rearrange("p (e r) -> p e r", e=8),
                                       wb, ALU.mult)
            gateB[i] = DVE.inc(ins)

        if i >= 1:
            emit_lagged(i - 1)
    emit_lagged(NT_OWN - 1)

    @block.sync
    def _(sync):
        sync.wait_ge(d_out, dma["out"])

    stB.close()
    stW.close()
    st_p45.close()
    st_top.close()
    return nc


_CACHE = {}


def make_in_maps(inputs):
    x = _f32(inputs["L0_attn_hidden_states"])
    wq = _f32(inputs["L0_attn_wq"])
    wk = _f32(inputs["L0_attn_wk"])
    wv = _f32(inputs["L0_attn_wv"])
    wo = _f32(inputs["L0_attn_wo"])
    qscale = _f32(inputs["L0_attn_query_scale"])
    kscale = _f32(inputs["L0_attn_key_scale"])
    W_base = _f32(inputs["L0_ffn_W_base"])
    b_base = _f32(inputs["L0_ffn_b_base"])
    W_router = _f32(inputs["L0_ffn_W_router"])
    A_w = _f32(inputs["L0_ffn_A_w"])
    B_w = _f32(inputs["L0_ffn_B_w"])

    wbra = _f32(np.concatenate(
        [W_base.T, W_router.T, A_w.reshape(NEXP * RANK, E).T], axis=1))
    bflat = _f32(np.transpose(B_w, (0, 2, 1)).reshape(NEXP * RANK, E))
    b_bcast = _f32(b_base[None, :])
    qs_rep = _f32(np.tile(qscale, (128, H)))
    ks_rep = _f32(np.tile(kscale, (128, KVH)))
    inv_freq = 1.0 / (10000.0 ** (np.arange(0, D // 2, dtype=np.float64) / (D // 2)))

    in_maps = []
    for core in range(NCORE):
        b = core // 4
        q4 = core % 4
        own0 = q4 * TOWN
        lo, hi = own0 - WIN, own0 + TOWN + WIN
        lo_c, hi_c = max(lo, 0), min(hi, S)
        xs = x[b, lo_c:hi_c, :]
        pad_l, pad_r = lo_c - lo, hi - hi_c
        if pad_l or pad_r:
            xs = np.concatenate(
                [np.repeat(xs[:1], pad_l, 0), xs, np.repeat(xs[-1:], pad_r, 0)], 0)
        pos = np.clip(np.arange(lo, hi, dtype=np.float64), 0, S - 1)
        ang = np.outer(pos, inv_freq)
        cos_t = _f32(np.concatenate([np.cos(ang), np.cos(ang)], 1))
        sin_t = _f32(np.concatenate([-np.sin(ang), np.sin(ang)], 1))
        vb = np.zeros((NT_OWN, NCH), dtype=np.float64)
        for qt in range(NT_OWN):
            for c in range(NCH):
                kb = own0 - WIN + 128 * (qt + c)
                vb[qt, c] = 1.0 if (0 <= kb <= S - 128) else 0.0
        vb = _f32(np.broadcast_to(vb.reshape(1, -1), (128, NT_OWN * NCH)))
        in_maps.append({
            "xh": _f32(xs), "cos_t": cos_t, "sin_t": sin_t, "vb": vb,
            "wq": wq, "wk": wk, "wv": wv, "wo": wo,
            "wbra": wbra, "bflat": bflat, "b_bcast": b_bcast,
            "qscale_rep": qs_rep, "kscale_rep": ks_rep,
        })
    return in_maps


def kernel(**inputs):
    in_maps = make_in_maps(inputs)
    if "nc" not in _CACHE:
        _CACHE["nc"] = build_program()
    nc = _CACHE["nc"]
    res = run_bass_kernel_spmd(nc, in_maps, core_ids=list(range(NCORE)))
    outp = np.empty((B, S, E), dtype=np.float32)
    for core in range(NCORE):
        b = core // 4
        q4 = core % 4
        outp[b, q4 * TOWN:(q4 + 1) * TOWN, :] = res.results[core]["out"]
    return outp
